# version 05
# baseline (speedup 1.0000x reference)
import contextlib
import numpy as np
import ml_dtypes
import concourse.bass as bass
import concourse.mybir as mybir
from concourse.bass_utils import run_bass_kernel_spmd

F32 = mybir.dt.float32
BF16 = mybir.dt.bfloat16
AF = mybir.ActivationFunctionType
AX = mybir.AxisListType
OP = mybir.AluOpType

S = 2048
DM = 4096
NKV = 8
RQ = 4
FF = 11008
NDC = DM // 128
NFC = FF // 128
NTT = 4
TT = 512
NQT = S // 128
EPS = 1e-5
QSCALE = float(1.0 / np.sqrt(128.0))
EXPC = 10.0


class Prog:
    def __init__(self, sems):
        self.recs = []
        self.cnt = {k: 0 for k in sems}
        self.tag = ""

    def op(self, eng, fn, waits=(), inc=None, inc_n=1):
        ws = tuple((s, v) for (s, v) in waits if v and v > 0)
        self.recs.append((eng, ws, fn, inc, inc_n, self.tag))
        if inc is not None:
            self.cnt[inc] += inc_n
            return self.cnt[inc]
        return None

    def at(self, sem):
        return self.cnt[sem]

    def emit(self, eng_name, eng, sems):
        waited = {}
        for (e, ws, fn, inc, inc_n, tag) in self.recs:
            if e != eng_name:
                continue
            for (s, v) in ws:
                if waited.get(s, 0) < v:
                    eng.wait_ge(sems[s], v)
                    waited[s] = v
            inst = fn(eng)
            if tag:
                inst.annotate(tag)
            if inc is not None:
                inst.then_inc(sems[inc], inc_n)


def _host_layout(inputs):
    x = np.asarray(inputs["L0_attn_hidden_states"], dtype=np.float32)

    def proj_slabs(w, nob):
        a = np.asarray(w, dtype=np.float32).reshape(NDC, 128, nob, 128)
        return np.ascontiguousarray(a.transpose(2, 1, 0, 3)).reshape(nob * 128, NDC * 128)

    def ffn_in_slabs(w):
        a = np.asarray(w, dtype=np.float32).reshape(NFC, 128, NDC, 128).transpose(0, 3, 2, 1)
        return np.ascontiguousarray(a).reshape(NFC * 128, NDC * 128)

    def ffn_out_slabs(w):
        a = np.asarray(w, dtype=np.float32).reshape(NDC, 128, NFC, 128).transpose(0, 3, 2, 1)
        return np.ascontiguousarray(a).reshape(NDC * 128, NFC * 128)

    shared = {
        "wq": proj_slabs(inputs["L0_attn_Wq"], 32),
        "wk": proj_slabs(inputs["L0_attn_Wk"], 8),
        "wv": proj_slabs(inputs["L0_attn_Wv"], 8),
        "wo": proj_slabs(inputs["L0_attn_Wo"], 32),
        "wg": ffn_in_slabs(inputs["L0_ffn_W_gate"]),
        "wu": ffn_in_slabs(inputs["L0_ffn_W_up"]),
        "wd": ffn_out_slabs(inputs["L0_ffn_W_down"]),
        "bq": np.ascontiguousarray(inputs["L0_attn_bq"], dtype=np.float32).reshape(1, DM),
        "bk": np.ascontiguousarray(inputs["L0_attn_bk"], dtype=np.float32).reshape(1, 1024),
        "bv": np.ascontiguousarray(inputs["L0_attn_bv"], dtype=np.float32).reshape(1, 1024),
        "bo": np.ascontiguousarray(inputs["L0_attn_bo"], dtype=np.float32).reshape(1, DM),
    }
    in_maps = []
    for c in range(8):
        m = dict(shared)
        m["x"] = np.ascontiguousarray(x[c].T)
        in_maps.append(m)
    return in_maps


def _consts():
    half = 64
    inv = 1.0 / (10000.0 ** (np.arange(half, dtype=np.float64) / half))
    ang = np.arange(S, dtype=np.float64)[None, :] * inv[:, None]
    cos = np.cos(ang).astype(np.float32)
    sin = np.sin(ang).astype(np.float32)
    cos_d = np.concatenate([cos, cos], 0).astype(ml_dtypes.bfloat16)
    sin_d = np.concatenate([sin, sin], 0).astype(ml_dtypes.bfloat16)
    ident = np.eye(128, dtype=ml_dtypes.bfloat16)
    q_ = np.arange(128)[:, None]
    k_ = np.arange(128)[None, :]
    tri = np.where(k_ <= q_, 0.0, -1e30).astype(np.float32)
    return cos_d, sin_d, ident, tri


def build_nc(phase_limit=None):
    nc = bass.Bass("TRN2", target_bir_lowering=False)
    es = contextlib.ExitStack()

    x_in = nc.dram_tensor("x", [DM, S], F32, kind="ExternalInput").ap()
    wq_in = nc.dram_tensor("wq", [32 * 128, NDC * 128], F32, kind="ExternalInput").ap()
    wk_in = nc.dram_tensor("wk", [8 * 128, NDC * 128], F32, kind="ExternalInput").ap()
    wv_in = nc.dram_tensor("wv", [8 * 128, NDC * 128], F32, kind="ExternalInput").ap()
    wo_in = nc.dram_tensor("wo", [32 * 128, NDC * 128], F32, kind="ExternalInput").ap()
    wg_in = nc.dram_tensor("wg", [NFC * 128, NDC * 128], F32, kind="ExternalInput").ap()
    wu_in = nc.dram_tensor("wu", [NFC * 128, NDC * 128], F32, kind="ExternalInput").ap()
    wd_in = nc.dram_tensor("wd", [NDC * 128, NFC * 128], F32, kind="ExternalInput").ap()
    bq_in = nc.dram_tensor("bq", [1, DM], F32, kind="ExternalInput").ap()
    bk_in = nc.dram_tensor("bk", [1, 1024], F32, kind="ExternalInput").ap()
    bv_in = nc.dram_tensor("bv", [1, 1024], F32, kind="ExternalInput").ap()
    bo_in = nc.dram_tensor("bo", [1, DM], F32, kind="ExternalInput").ap()
    out = nc.dram_tensor("out", [DM, S], F32, kind="ExternalOutput").ap()

    ctx_scr = nc.dram_tensor("ctx_scr", [NDC * 128, S], BF16, kind="Internal").ap()
    ap_scr = nc.dram_tensor("ap_scr", [NDC * 128, S], BF16, kind="Internal").ap()
    xp_scr = nc.dram_tensor("xp_scr", [NDC * 128, S], BF16, kind="Internal").ap()
    m_scr = nc.dram_tensor("m_scr", [NTT, 128, NFC * TT], BF16, kind="Internal").ap()
    wd_scr = nc.dram_tensor("wd_scr", [NDC * 128, NFC * 128], BF16, kind="Internal").ap()

    cos_np, sin_np, ident_np, tri_np = _consts()
    cos_t = nc.inline_tensor(cos_np, name="cos_t")
    sin_t = nc.inline_tensor(sin_np, name="sin_t")
    ident_t = nc.inline_tensor(ident_np, name="ident_t")
    tri_t = nc.inline_tensor(tri_np, name="tri_t")
    trib16_t = nc.inline_tensor(tri_np.astype(ml_dtypes.bfloat16), name="trib16_t")

    identb = es.enter_context(nc.sbuf_tensor("identb", [128, 128], BF16))
    cosb = es.enter_context(nc.sbuf_tensor("cosb", [128, S], BF16))
    sinb = es.enter_context(nc.sbuf_tensor("sinb", [128, S], BF16))
    trib = es.enter_context(nc.sbuf_tensor("trib", [128, 128], F32))
    trib16 = es.enter_context(nc.sbuf_tensor("trib16", [128, 128], BF16))
    onesr = es.enter_context(nc.sbuf_tensor("onesr", [1, TT], BF16))
    negb = es.enter_context(nc.sbuf_tensor("negb", [1, 128], BF16))
    onescolb = es.enter_context(nc.sbuf_tensor("onescolb", [128, 1], BF16))

    ar_bytes = nc.sbuf_bytes_remaining - 128
    ar_elems = (ar_bytes // 512) * 256
    arena = es.enter_context(nc.sbuf_tensor("arena", [128, ar_elems], BF16))
    AB = ar_elems * 2

    def av(off, part, elems, dtype=BF16):
        assert off % 4 == 0
        if dtype == BF16:
            return arena[0:part, off // 2: off // 2 + elems]
        assert dtype == F32
        return arena[0:part, off // 2: off // 2 + 2 * elems].bitcast(F32)

    ps7 = es.enter_context(nc.psum_tensor("ps7", [128, 3584], F32))
    tpb = es.enter_context(nc.psum_tensor("tpb", [128, 1024], BF16))
    R0 = ps7[:, 0:1024]
    R1 = ps7[:, 1024:2048]
    UT = ps7[:, 2048:2560]
    CXA = ps7[:, 2560:3072]
    CXB = ps7[:, 3072:3584]

    dma_sems = ["scst", "sx", "swb0", "swb1", "sbias", "sctx", "scxr",
                "sap0", "sap1", "sab0", "sab1", "sxf0", "sxf1", "sxp0", "sxp1",
                "sg0", "sg1", "su0", "su1", "scv", "scw", "sm0", "sm1",
                "smt", "swd0", "swd1", "sx70", "sx71", "so0", "so1", "sdbg"]
    sem_names = ["pe", "act", "dve", "pool"] + dma_sems
    sems = {s: es.enter_context(nc.semaphore(s)) for s in sem_names}
    P = Prog(sem_names)

    def dma(queue, sem, dst, src, waits=()):
        eng = "pool" if queue == "qgp" else "sp"
        return P.op(eng, lambda e, d=dst, s=src: e.dma_start(d, s), waits=waits,
                    inc=sem, inc_n=16)

    def mm(dst, lhsT, rhs, start, stop, waits=(), inc=None):
        return P.op("pe", lambda e, d=dst, l=lhsT, r=rhs, st=start, sp=stop:
                    e.matmul(d, l, r, start=st, stop=sp), waits=waits, inc=inc)

    def dve(fn, waits=()):
        return P.op("dve", fn, waits=waits, inc="dve")

    def act(fn, waits=()):
        return P.op("act", fn, waits=waits, inc="act")

    dma("qsp", "scst", identb[:, :], ident_t.ap())
    dma("qsp", "scst", cosb[:, :], cos_t.ap())
    dma("qsp", "scst", sinb[:, :], sin_t.ap())
    dma("qsp", "scst", trib[:, :], tri_t.ap())
    dma("qsp", "scst", trib16[:, :], trib16_t.ap())
    c_const = ("scst", P.at("scst"))
    P.op("pool", lambda e: e.memset(onesr[:, :], 1.0), inc="pool")
    P.op("pool", lambda e: e.memset(negb[:, :], -1.0), inc="pool")
    P.op("pool", lambda e: e.memset(onescolb[:, :], 1.0), inc="pool")
    c_mem = P.at("pool")

    o_xh = 0
    o_wbuf = 131072
    o_kvq = 147456
    o_psb = 163840
    o_ptsb = 172032
    o_ctxh = 176128
    o_ropeA = 180224
    o_ropeB = 182272
    o_rowa = 184320
    o_rowb = 188416
    o_nbc = 192512
    o_rs = o_nbc + 64
    o_rc = o_nbc + 80
    o_kmaxf = o_nbc + 96
    o_bkr = o_nbc + 128
    o_bvr = o_nbc + 384
    o_bqr = o_nbc + 640
    o_m2 = o_nbc + 1664
    o_rcb = o_m2 + 4160
    assert o_rcb + 2048 <= AB, f"arena {AB} < {o_rcb + 2048}"

    def xh(dc):
        return arena[:, (o_xh // 2) + dc * S:(o_xh // 2) + (dc + 1) * S]

    rowa = av(o_rowa, 1, S)
    rowb = av(o_rowb, 1, S)
    m2row = av(o_m2, 1, S)
    kmaxf = av(o_kmaxf, 1, 1, F32)

    P.tag = "p1"
    mbt = av(o_wbuf, 128, S)
    rbt = av(o_wbuf + 4096, 128, S)
    sqsc = [av(o_kvq + i * 1024, 128, TT) for i in range(2)]

    for dc in range(NDC):
        dma("qgp", "sx", xh(dc), x_in[dc * 128:(dc + 1) * 128, :])
    x_all = P.at("sx")

    st_ev = 0
    sq_slot_pe = [0, 0]
    for tt in range(NTT):
        sum_ps = ps7[0:1, 0:512]
        sq_ps = ps7[0:1, 512:1024]
        for dc in range(NDC):
            wv_ = [("sx", x_all), ("pool", c_mem), ("dve", st_ev)]
            mm(sum_ps, onescolb[:, :], xh(dc)[:, tt * TT:(tt + 1) * TT],
               start=(dc == 0), stop=(dc == NDC - 1), waits=wv_,
               inc="pe" if dc == NDC - 1 else None)
            sl = sqsc[dc % 2]
            dv = dve(lambda e, s=sl, d=dc, t=tt: e.tensor_tensor(
                s, xh(d)[:, t * TT:(t + 1) * TT], xh(d)[:, t * TT:(t + 1) * TT],
                OP.mult), waits=[("sx", x_all), ("pe", sq_slot_pe[dc % 2])])
            pv = mm(sq_ps, onescolb[:, :], sl, start=(dc == 0), stop=(dc == NDC - 1),
                    waits=[("dve", dv), ("dve", st_ev)], inc="pe")
            sq_slot_pe[dc % 2] = pv
        pe_tt = P.at("pe")
        d1 = dve(lambda e, t=tt: e.tensor_copy(rowa[0:1, t * TT:(t + 1) * TT],
                 ps7[0:1, 0:512]), waits=[("pe", pe_tt)])
        d2 = dve(lambda e, t=tt: e.tensor_copy(rowb[0:1, t * TT:(t + 1) * TT],
                 ps7[0:1, 512:1024]), waits=[("pe", pe_tt)])
        st_ev = d2

    def row_norm_stats(dep_dve, n=S, m2_waits=()):
        d_m = dve(lambda e: e.tensor_scalar(rowa[0:1, 0:n], rowa[0:1, 0:n],
                  1.0 / DM, None, OP.mult), waits=[("dve", dep_dve)])
        d_m2 = dve(lambda e: e.tensor_tensor(m2row[0:1, 0:n], rowa[0:1, 0:n],
                   rowa[0:1, 0:n], OP.mult), waits=[("dve", d_m)] + list(m2_waits))
        d_v = dve(lambda e: e.scalar_tensor_tensor(rowb[0:1, 0:n], rowb[0:1, 0:n],
                  1.0 / DM, m2row[0:1, 0:n], OP.mult, OP.subtract),
                  waits=[("dve", d_m2), ("dve", dep_dve)])
        d_ve = dve(lambda e: e.tensor_scalar(rowb[0:1, 0:n], rowb[0:1, 0:n],
                   EPS, None, OP.add), waits=[("dve", d_v)])
        a_l = act(lambda e: e.activation(rowb[0:1, 0:n], rowb[0:1, 0:n], AF.Ln,
                  bias=0.0, scale=1.0), waits=[("dve", d_ve)])
        a_e = act(lambda e: e.activation(rowb[0:1, 0:n], rowb[0:1, 0:n], AF.Exp,
                  bias=0.0, scale=-0.5), waits=[("act", a_l)])
        return a_e

    a_rstd = row_norm_stats(st_ev)

    def bcast_rows(mdst, rdst, a_dep, blocks=NTT, bw=TT, extra=()):
        ut_free = P.at("act")
        last_a = None
        for (row, dst) in ((rowa, mdst), (rowb, rdst)):
            for b in range(blocks):
                pv = mm(UT, negb[0:1, 0:128], row[0:1, b * bw:(b + 1) * bw],
                        start=True, stop=True,
                        waits=[("act", a_dep), ("act", ut_free), ("pool", c_mem)],
                        inc="pe")
                last_a = act(lambda e, b=b, d=dst, w=bw: e.activation(
                    d[:, b * w:(b + 1) * w], UT[:, 0:w], AF.Copy, bias=0.0,
                    scale=-1.0), waits=[("pe", pv)] + list(extra))
                ut_free = last_a
        return last_a

    a_bc1 = bcast_rows(mbt, rbt, a_rstd)

    tscr = [av(o_psb, 128, S), av(o_psb + 4096, 128, S)]
    h_marks = []
    for dc in range(NDC):
        sl = tscr[dc % 2]
        wv_ = [("act", a_bc1)]
        if dc >= 2:
            wv_.append(("dve", h_marks[dc - 2]))
        d1 = dve(lambda e, d=dc, s=sl: e.tensor_tensor(s, xh(d), mbt[:, :],
                 OP.subtract), waits=wv_)
        d2 = dve(lambda e, d=dc, s=sl: e.tensor_tensor(xh(d), s, rbt[:, :], OP.mult),
                 waits=[("dve", d1)])
        h_marks.append(d2)
    h_all = h_marks[-1]

    if phase_limit == "p1":
        for dc in range(NDC):
            dma("qgp", "sdbg", out[dc * 128:(dc + 1) * 128, :], xh(dc),
                waits=[("dve", h_all)])
        return _finish(nc, es, P, sems)

    P.tag = "p23setup"
    wbuf = [av(o_wbuf, 128, 4096), av(o_wbuf + 8192, 128, 4096)]
    kK = av(o_kvq, 128, S)
    kV = av(o_kvq + 4096, 128, S)
    kQ = [av(o_kvq + 8192, 128, S), av(o_kvq + 12288, 128, S)]
    psb = [av(o_psb, 128, S), av(o_psb + 4096, 128, S)]
    psbh = [av(o_psb + j * 2048, 128, 1024) for j in range(4)]
    pth = [av(o_ptsb + j * 2048, 128, 1024) for j in range(2)]
    ctxh = av(o_ctxh, 128, S)
    ropeA = av(o_ropeA, 128, 1024)
    ropeB = av(o_ropeB, 128, 1024)
    nbc = av(o_nbc, 128, 16, F32)
    rs4 = av(o_rs, 128, 4, F32)
    rsb4 = av(o_rc, 128, 8)
    rs5 = av(o_nbc + 100, 128, 1, F32)
    rcb = av(o_rcb, 128, 512, F32)
    rrow = m2row
    bkr = av(o_bkr, 1, 128)
    bvr = av(o_bvr, 1, 128)
    bqr = av(o_bqr, 1, 512)

    wb_free = [[("dve", h_all)], [("dve", h_all)]]
    wb_loaded = [0, 0]
    slot_turn = [0]
    acc_free = [[("dve", 0)], [("dve", 0)]]
    reg_free = [[("dve", 0)], [("dve", 0)]]
    reg_turn = [0]
    tp_free = [("act", 0)]
    pt_free = [("pe", 0)]
    pbuf_free = [[("dve", h_all)], [("dve", h_all)]]
    ctxh_free = [("qsp", 0)]
    nbc_free = [("act", 0)]
    rowa_free = [("act", a_bc1)]
    rowb_free = [("act", a_bc1)]
    scrA_free = [("dve", 0)]
    scrB_free = [("dve", 0)]
    kmax_mark = [0]
    kv_mark = [0]
    ctx_evac = {0: ("dve", 0), 1: ("dve", 0)}
    q_slot_attn_pe = [0, 0]
    rs_cast = [0]
    rs_done = {0: 0, 1: 0, 2: 0, 3: 0}
    pring = [0]
    ptring = [0]
    ph_free = [[("dve", h_all)] for _ in range(4)]
    ptf_free = {0: ("pe", 0), 1: ("pe", 0)}
    ut_mm = [0]
    ut_free_a = [("act", 0)]
    rrow_free = [("pe", 0)]
    rcb_free = [("dve", 0)]

    def load_slab(src_ap):
        sl = slot_turn[0] % 2
        slot_turn[0] += 1
        sname = "swb%d" % sl
        v = dma("qgp", sname, wbuf[sl][:, :], src_ap, waits=wb_free[sl])
        wb_loaded[sl] = (sname, v)
        return sl

    acc_turn = [0]

    def proj_chunk(sl, bias_row, evac_fn, extra=()):
        last_pv = None
        for tt in range(NTT):
            bi = acc_turn[0] % 2
            acc_turn[0] += 1
            bank = [CXA, CXB][bi]
            for kc in range(NDC):
                wv_ = [wb_loaded[sl], ("dve", h_marks[kc])] + list(extra)
                if kc == 0:
                    wv_ = wv_ + acc_free[bi]
                mm(bank, wbuf[sl][:, kc * 128:(kc + 1) * 128],
                   xh(kc)[:, tt * TT:(tt + 1) * TT], start=(kc == 0), stop=False,
                   waits=wv_)
            pv = mm(bank, bias_row, onesr[0:1, 0:TT], start=False, stop=True,
                    waits=[("pool", c_mem)], inc="pe")
            mk = evac_fn(bank, pv, tt)
            acc_free[bi] = [("dve", mk)]
            last_pv = pv
        return last_pv

    def rope_inplace(dst, pre_waits=()):
        e_last = None
        for hf in range(2):
            cs = slice(hf * 1024, hf * 1024 + 1024)
            wv0 = scrA_free + scrB_free + list(pre_waits) + [c_const]
            dve(lambda e, cs=cs: e.tensor_tensor(ropeA[0:64, :], dst[0:64, cs],
                cosb[0:64, cs], OP.mult), waits=wv0)
            dve(lambda e, cs=cs: e.tensor_tensor(ropeA[64:128, :], dst[0:64, cs],
                sinb[0:64, cs], OP.mult))
            dve(lambda e, cs=cs: e.tensor_tensor(ropeB[0:64, :], dst[64:128, cs],
                sinb[64:128, cs], OP.mult))
            e4 = dve(lambda e, cs=cs: e.tensor_tensor(ropeB[64:128, :],
                     dst[64:128, cs], cosb[64:128, cs], OP.mult))
            dve(lambda e, cs=cs: e.tensor_tensor(dst[0:64, cs], ropeA[0:64, :],
                ropeB[0:64, :], OP.subtract), waits=[("dve", e4)])
            e_last = dve(lambda e, cs=cs: e.tensor_tensor(dst[64:128, cs],
                         ropeA[64:128, :], ropeB[64:128, :], OP.add),
                         waits=[("dve", e4)])
            scrA_free[0] = ("dve", e_last)
            scrB_free[0] = ("dve", e_last)
        return e_last

    def sq_row_mms(srcbuf, src_mark, extra=()):
        sq_slots = [ropeA[:, 0:TT], ropeB[:, 0:TT]]
        slot_w = [scrA_free, scrB_free]
        pv = None
        for tt in range(NTT):
            i = tt % 2
            dd = dve(lambda e, s=sq_slots[i], t=tt: e.tensor_tensor(
                s, srcbuf[:, t * TT:(t + 1) * TT], srcbuf[:, t * TT:(t + 1) * TT],
                OP.mult), waits=[("dve", src_mark)] + slot_w[i])
            pv = mm(ps7[0:1, tt * TT:(tt + 1) * TT], onescolb[:, :], sq_slots[i],
                    start=True, stop=True, waits=[("dve", dd)] + list(extra),
                    inc="pe")
            slot_w[i][0] = ("pe", pv)
        return pv

    def attention_head(g, hh, qbuf, q_mark):
        hg = 4 * g + hh
        P.tag = "attn_g%d_h%d" % (g, hh)
        pe_q2 = sq_row_mms(qbuf, q_mark,
                           extra=reg_free[0] + reg_free[1] + [rowa_free[0]])
        d_ev = dve(lambda e: e.tensor_copy(rowa[:, :], ps7[0:1, 0:2048]),
                   waits=[("pe", pe_q2), rowa_free[0]])
        d_u1 = dve(lambda e: e.tensor_scalar(rowa[:, :], rowa[:, :],
                   kmaxf[0:1, 0:1], 1e-20, OP.mult, OP.add),
                   waits=[("dve", d_ev), ("dve", kmax_mark[0])])
        a_u2 = act(lambda e: e.activation(rowa[:, :], rowa[:, :], AF.Ln, bias=0.0,
                   scale=1.0), waits=[("dve", d_u1)])
        a_u3 = act(lambda e: e.activation(rowa[:, :], rowa[:, :], AF.Exp, bias=0.0,
                   scale=0.5), waits=[("act", a_u2)])
        nb_mm = None
        for qt in range(NQT):
            nb_mm = mm(ps7[:, 2048 + qt:2049 + qt],
                       rowa[0:1, qt * 128:(qt + 1) * 128], negb[0:1, 0:1],
                       start=True, stop=True,
                       waits=[("act", a_u3), ("pool", c_mem), nbc_free[0],
                              ut_free_a[0]],
                       inc="pe" if qt == NQT - 1 else None)
        rowa_free[0] = ("pe", nb_mm)
        a_nb = act(lambda e: e.activation(nbc[:, :], ps7[:, 2048:2064], AF.Copy,
                   bias=EXPC, scale=1.0), waits=[("pe", nb_mm)])
        nbc_free[0] = ("act", a_nb)

        for qt in range(NQT):
            nkc = qt + 1
            P.tag = "attn_g%d_h%d_q%d" % (g, hh, qt)
            ri = qt % 4
            rs_c = rs4[:, ri:ri + 1]
            halves = [(0, min(nkc, 8))] + ([(8, nkc)] if nkc > 8 else [])
            qg = qt // 4
            cb = qg % 2
            bank = [CXA, CXB][cb]
            pv_last = None
            for (hi, (kc0, kc1)) in enumerate(halves):
                rgi = reg_turn[0] % 2
                reg_turn[0] += 1
                reg = [R0, R1][rgi]
                pj = pring[0] % 4
                pring[0] += 1
                ptile = psbh[pj]
                wv_ = (reg_free[rgi] + [("dve", q_mark), ("dve", kv_mark[0]),
                                        ("dve", d_ev), c_const])
                has_diag = (kc1 - 1 == qt)
                lastq = None
                nc_cols = (kc1 - kc0) * 128
                for c0 in range(0, nc_cols, 512):
                    cw = min(512, nc_cols - c0)
                    last_chunk = (c0 + cw == nc_cols)
                    lastq = mm(reg[:, c0:c0 + cw],
                               qbuf[:, qt * 128:(qt + 1) * 128],
                               kK[:, kc0 * 128 + c0:kc0 * 128 + c0 + cw],
                               start=True, stop=not (has_diag and last_chunk),
                               waits=wv_,
                               inc=None if (has_diag or not last_chunk) else "pe")
                    wv_ = []
                if has_diag:
                    w = (qt - kc0) * 128
                    lastq = mm(reg[:, w:w + 128], identb[:, :], trib16[:, :],
                               start=False, stop=True, waits=[], inc="pe")
                else:
                    lastq = P.at("pe")
                wcols = nc_cols
                rcol = rs_c if hi == 0 else rs5
                ewait = [("pe", lastq), ("act", a_nb), ("dve", rs_cast[0])] \
                    + ph_free[pj]
                av_ = act(lambda e, r=reg, p=ptile, w=wcols, q=qt, rc=rcol:
                          e.activation(p[:, 0:w], r[:, 0:w],
                          AF.Exp, bias=nbc[:, q:q + 1], scale=1.0, accum_out=rc),
                          waits=ewait)
                reg_free[rgi] = [("act", av_)]
                tj = ptring[0] % 2
                ptring[0] += 1
                pts = pth[tj]
                t_last = None
                for kc in range(kc0, kc1):
                    t_last = P.op("pe", lambda e, p=ptile, k=(kc - kc0):
                                  e.transpose(tpb[:, k * 128:(k + 1) * 128],
                                  p[:, k * 128:(k + 1) * 128], identb[:, :]),
                                  waits=[("act", av_), c_const,
                                         tp_free[0], ptf_free[tj]],
                                  inc="pe" if kc == kc1 - 1 else None)
                ph_free[pj] = [("pe", t_last)]
                d_pt = dve(lambda e, n=nc_cols, d=pts: e.tensor_copy(
                    d[:, 0:n], tpb[:, 0:n]), waits=[("pe", t_last)])
                tp_free[0] = ("dve", d_pt)
                for kc in range(kc0, kc1):
                    wv_ = [("dve", d_pt), ("dve", kv_mark[0])]
                    if qt % 4 == 0 and hi == 0 and kc == 0:
                        wv_ = wv_ + acc_free[cb] + [ctx_evac[cb]]
                    pv_last = mm(bank[:, (qt % 4) * 128:(qt % 4 + 1) * 128],
                                 kV[:, kc * 128:(kc + 1) * 128],
                                 pts[:, (kc - kc0) * 128:(kc - kc0 + 1) * 128],
                                 start=(qt % 4 == 0 and hi == 0 and kc == 0),
                                 stop=(qt % 4 == 3 and kc == nkc - 1),
                                 waits=wv_, inc="pe" if kc == kc1 - 1 else None)
                ptf_free[tj] = ("pe", pv_last)
            if len(halves) == 2:
                d_rs = dve(lambda e, a=rs_c: e.tensor_tensor(
                    a, a, rs5, OP.add), waits=[("act", reg_free[0][0][1] if False
                    else av_), ("act", av_)])
            else:
                d_rs = dve(lambda e: e.engine_nop(), waits=[("act", av_)])
            rs_done[ri] = d_rs
            if qt % 4 == 3:
                d_cast = dve(lambda e: e.tensor_copy(rsb4[:, 0:4], rs4[:, 0:4]),
                             waits=[("dve", rs_done[0]), ("dve", rs_done[1]),
                                    ("dve", rs_done[2]), ("dve", rs_done[3]),
                                    ("pe", ut_mm[0])])
                rs_cast[0] = d_cast
                t_mm = None
                for j in range(4):
                    t_mm = mm(UT[0:1, j * 128:(j + 1) * 128], rsb4[:, j:j + 1],
                              identb[:, :], start=True, stop=True,
                              waits=[("dve", d_cast), ut_free_a[0], nbc_free[0]],
                              inc="pe" if j == 3 else None)
                d_rr = dve(lambda e: e.reciprocal(rcb[0:1, 0:512],
                           UT[0:1, 0:512]), waits=[("pe", t_mm), rcb_free[0]])
                d_rr2 = dve(lambda e: e.tensor_copy(rrow[0:1, 0:512],
                            rcb[0:1, 0:512]), waits=[("dve", d_rr), rrow_free[0]])
                b_mm = mm(UT, negb[0:1, 0:128], rrow[0:1, 0:512], start=True,
                          stop=True, waits=[("dve", d_rr2)], inc="pe")
                ut_mm[0] = b_mm
                a_rcb = act(lambda e: e.activation(rcb[:, :], UT[:, 0:512],
                            AF.Copy, bias=0.0, scale=-1.0),
                            waits=[("pe", b_mm), rcb_free[0]])
                ut_free_a[0] = ("act", a_rcb)
                rrow_free[0] = ("pe", b_mm)
                d_cx = dve(lambda e, c=bank, q0=qg: e.tensor_tensor(
                    ctxh[:, q0 * TT:(q0 + 1) * TT], c, rcb[:, :], OP.mult),
                    waits=[("pe", pv_last), ("act", a_rcb), ctxh_free[0]])
                rcb_free[0] = ("dve", d_cx)
                ctx_evac[cb] = ("dve", d_cx)
                acc_free[cb] = [("dve", d_cx)]
        v = dma("qsp", "sctx", ctx_scr[hg * 128:(hg + 1) * 128, :], ctxh[:, :],
                waits=[ctx_evac[0], ctx_evac[1]])
        ctxh_free[0] = ("sctx", v)
        return v

    ctx_spills = []
    for g in range(NKV):
        P.tag = "grp%d" % g
        dma("qgp", "sbias", bkr, bk_in[0:1, g * 128:(g + 1) * 128],
            waits=[("pe", q_slot_attn_pe[0]), ("pe", q_slot_attn_pe[1])])
        dma("qgp", "sbias", bvr, bv_in[0:1, g * 128:(g + 1) * 128])
        dma("qgp", "sbias", bqr, bq_in[0:1, g * 512:(g + 1) * 512])
        bias_q = P.at("sbias")

        sl = load_slab(wk_in[g * 128:(g + 1) * 128, :])

        def evac_k(bank, pv, tt):
            return dve(lambda e, b=bank, t=tt: e.tensor_copy(
                kK[:, t * TT:(t + 1) * TT], b),
                waits=[("pe", pv), ("pe", q_slot_attn_pe[0]),
                       ("pe", q_slot_attn_pe[1])])

        lp = proj_chunk(sl, bkr[0:1, :], evac_k, extra=[("sbias", bias_q)])
        wb_free[sl] = [("pe", lp)]
        d_krope = rope_inplace(kK, pre_waits=[("pe", q_slot_attn_pe[0]),
                                              ("pe", q_slot_attn_pe[1])])

        sl = load_slab(wv_in[g * 128:(g + 1) * 128, :])
        vf = psb[0]

        def evac_v(bank, pv, tt):
            return dve(lambda e, b=bank, t=tt: e.tensor_copy(
                vf[:, t * TT:(t + 1) * TT], b),
                waits=[("pe", pv)] + ph_free[0] + ph_free[1])

        lp = proj_chunk(sl, bvr[0:1, :], evac_v, extra=[("sbias", bias_q)])
        wb_free[sl] = [("pe", lp)]
        v_fm = P.at("dve")
        v_tr = []
        t_last = None
        for b0 in range(0, 16, 8):
            for kc in range(b0, b0 + 8):
                t_last = P.op("pe", lambda e, k=kc, j=(kc - b0): e.transpose(
                    tpb[:, j * 128:(j + 1) * 128], vf[:, k * 128:(k + 1) * 128],
                    identb[:, :]),
                    waits=[("dve", v_fm), c_const, tp_free[0], pt_free[0]],
                    inc="pe" if kc == b0 + 7 else None)
            a_v = act(lambda e, b0=b0: e.copy(kV[:, b0 * 128:(b0 + 8) * 128],
                      tpb[:, 0:1024]), waits=[("pe", t_last)])
            tp_free[0] = ("act", a_v)
            v_tr.append(a_v)
        ph_free[0] = [("pe", t_last)]
        ph_free[1] = [("pe", t_last)]

        d_krj = dve(lambda e: e.engine_nop(), waits=[("dve", d_krope)])
        pe_k2 = sq_row_mms(kK, d_krj,
                           extra=reg_free[0] + reg_free[1] + [rowb_free[0]])
        d_ke = dve(lambda e: e.tensor_copy(rowb[:, :], ps7[0:1, 0:2048]),
                   waits=[("pe", pe_k2), rowb_free[0]])
        d_km = dve(lambda e: e.reduce_max(kmaxf[0:1, 0:1], rowb[:, :], axis=AX.X),
                   waits=[("dve", d_ke)])
        kmax_mark[0] = d_km
        rowb_free[0] = ("dve", d_km)
        d_kv = dve(lambda e: e.engine_nop(),
                   waits=[("dve", d_krope), ("act", v_tr[-1])])
        kv_mark[0] = d_kv

        for hh in range(RQ):
            qs = hh % 2
            qb = kQ[qs]
            sl = load_slab(wq_in[(4 * g + hh) * 128:(4 * g + hh + 1) * 128, :])

            def evac_q(bank, pv, tt, q=qb, qs=qs):
                return dve(lambda e, b=bank, t=tt, q=q: e.tensor_scalar(
                    q[:, t * TT:(t + 1) * TT], b, QSCALE, None, OP.mult),
                    waits=[("pe", pv), ("pe", q_slot_attn_pe[qs])])

            lp = proj_chunk(sl, bqr[0:1, hh * 128:(hh + 1) * 128], evac_q,
                            extra=[("sbias", bias_q)])
            wb_free[sl] = [("pe", lp)]
            d_qrope = rope_inplace(qb, pre_waits=[("pe", q_slot_attn_pe[qs])])
            v = attention_head(g, hh, qb, d_qrope)
            ctx_spills.append(v)
            q_slot_attn_pe[qs] = P.at("pe")

        rowb_free[0] = ("dve", P.at("dve"))

    pe_attn_end = P.at("pe")
    dve_attn_end = P.at("dve")
    act_attn_end = P.at("act")

    P.tag = "p4"
    for dc in range(NDC):
        dma("qsp", "scxr", xh(dc), ctx_scr[dc * 128:(dc + 1) * 128, :],
            waits=[("pe", pe_attn_end), ("dve", dve_attn_end),
                   ("sctx", P.at("sctx"))])
    ctx_rd_all = P.at("scxr")
    bor = av(o_kvq, 1, DM)
    v_bo = dma("qgp", "sbias", bor, bo_in[0:1, :], waits=[("pe", pe_attn_end)])
    apst = [psb[0], psb[1]]
    ap_spill = []
    apst_free = [[("act", act_attn_end), ("pe", pe_attn_end)],
                 [("act", act_attn_end), ("pe", pe_attn_end)]]
    for ob in range(NDC):
        sl = load_slab(wo_in[ob * 128:(ob + 1) * 128, :])
        st = ob % 2
        dvl = None
        for tt in range(NTT):
            bi = acc_turn[0] % 2
            acc_turn[0] += 1
            bank = [CXA, CXB][bi]
            for kc in range(NDC):
                wv_ = [wb_loaded[sl], ("scxr", ctx_rd_all), ("sbias", v_bo)]
                if kc == 0:
                    wv_ = wv_ + acc_free[bi]
                mm(bank, wbuf[sl][:, kc * 128:(kc + 1) * 128],
                   xh(kc)[:, tt * TT:(tt + 1) * TT], start=(kc == 0), stop=False,
                   waits=wv_)
            pv = mm(bank, bor[0:1, ob * 128:(ob + 1) * 128], onesr[0:1, 0:TT],
                    start=False, stop=True, inc="pe")
            dvl = dve(lambda e, b=bank, t=tt, s=st: e.tensor_copy(
                apst[s][:, t * TT:(t + 1) * TT], b),
                waits=[("pe", pv)] + apst_free[st])
            acc_free[bi] = [("dve", dvl)]
            wb_free[sl] = [("pe", pv)]
        v = dma("qsp", "sap%d" % st, ap_scr[ob * 128:(ob + 1) * 128, :],
                apst[st][:, :], waits=[("dve", dvl)])
        ap_spill.append(v)
        apst_free[st] = [("sap%d" % st, v)]

    pe_p4_end = P.at("pe")
    ap_all = [("sap0", P.at("sap0")), ("sap1", P.at("sap1"))]

    P.tag = "p5"
    o5_apb = o_wbuf
    o5_sqs = o_wbuf + 8192
    o5_xsb = o_wbuf + 10240
    o5_tsc = o_wbuf + 26624
    o5_xps = o_wbuf + 34816
    o5_mb = o_wbuf + 43008
    o5_rb = o_wbuf + 47104
    assert o_wbuf + 51200 <= o_rowa
    apb = [av(o5_apb, 128, S), av(o5_apb + 4096, 128, S)]
    sqs5 = [av(o5_sqs, 128, TT), av(o5_sqs + 1024, 128, TT)]
    xsb = [av(o5_xsb, 128, S, F32), av(o5_xsb + 8192, 128, S, F32)]
    tsc5 = [av(o5_tsc, 128, S), av(o5_tsc + 4096, 128, S)]
    xps5 = [av(o5_xps, 128, S), av(o5_xps + 4096, 128, S)]
    mbt2 = av(o5_mb, 128, S)
    rbt2 = av(o5_rb, 128, S)

    def stats_ln2(pre_waits):
        apb2 = [av(o_kvq + 8192, 128, S), av(o_kvq + 12288, 128, S)]
        sqs2 = [av(o_ropeA, 128, TT), av(o_ropeB, 128, TT)]
        ev_last = 0
        buf_free = [list(pre_waits), list(pre_waits)]
        sq_free = [list(pre_waits), list(pre_waits)]
        for half in range(2):
            for dc in range(NDC):
                i = dc % 2
                v = dma("qsp", "sab%d" % i, apb2[i],
                        ap_scr[dc * 128:(dc + 1) * 128, :],
                        waits=buf_free[i] + [("sap%d" % (dc % 2), ap_spill[dc])])
                for tt in (half * 2, half * 2 + 1):
                    sum_ps = ps7[0:1, (tt % 2) * TT:(tt % 2 + 1) * TT]
                    sq_ps = ps7[0:1, 1024 + (tt % 2) * TT:1024 + (tt % 2 + 1) * TT]
                    wv_ = [("sab%d" % i, v)]
                    if dc == 0:
                        wv_ += list(pre_waits) + [("dve", ev_last)]
                    mm(sum_ps, onescolb[:, :], apb2[i][:, tt * TT:(tt + 1) * TT],
                       start=(dc == 0), stop=(dc == NDC - 1), waits=wv_,
                       inc="pe" if dc == NDC - 1 else None)
                    si = dc % 2
                    dd = P.op("pool", lambda e, s=sqs2[si], b=apb2[i], t=tt:
                              e.tensor_tensor(s, b[:, t * TT:(t + 1) * TT],
                              b[:, t * TT:(t + 1) * TT], OP.mult),
                              waits=[("sab%d" % i, v)] + sq_free[si], inc="pool")
                    pv2 = mm(sq_ps, onescolb[:, :], sqs2[si], start=(dc == 0),
                             stop=(dc == NDC - 1),
                             waits=[("pool", dd)] + (wv_ if dc == 0 else []),
                             inc="pe")
                    sq_free[si] = [("pe", pv2)]
                buf_free[i] = [("pe", P.at("pe"))]
            pe_h = P.at("pe")
            for tt in (half * 2, half * 2 + 1):
                dve(lambda e, t=tt: e.tensor_copy(rowa[0:1, t * TT:(t + 1) * TT],
                    ps7[0:1, (t % 2) * TT:(t % 2 + 1) * TT]), waits=[("pe", pe_h)])
                ev_last = dve(lambda e, t=tt: e.tensor_copy(
                    rowb[0:1, t * TT:(t + 1) * TT],
                    ps7[0:1, 1024 + (t % 2) * TT:1024 + (t % 2 + 1) * TT]),
                    waits=[("pe", pe_h)])
        return ev_last

    pre_w = [("pe", pe_attn_end), ("act", act_attn_end), ("dve", dve_attn_end),
             rowa_free[0], rowb_free[0]]
    s_ev = stats_ln2(pre_w)
    if phase_limit == "rawstats":
        dma("qgp", "sdbg", out[256:257, :], rowb[0:1, :], waits=[("dve", s_ev)])
        dma("qgp", "sdbg", out[257:258, :], rowa[0:1, :], waits=[("dve", s_ev)])
        return _finish(nc, es, P, sems)
    a_r2 = row_norm_stats(s_ev)
    a_bc2 = bcast_rows(mbt2, rbt2, a_r2,
                       extra=[("pe", pe_p4_end), ("sctx", P.at("sctx"))] + ap_all)
    rowa_free[0] = ("act", a_bc2)
    rowb_free[0] = ("act", a_bc2)

    if phase_limit == "ln2":
        dma("qgp", "sdbg", out[0:128, :], rbt2, waits=[("act", a_bc2)])
        dma("qgp", "sdbg", out[128:256, :], mbt2, waits=[("act", a_bc2)])
        dma("qgp", "sdbg", out[256:257, :], rowb[0:1, :], waits=[("act", a_bc2)])
        dma("qgp", "sdbg", out[257:258, :], rowa[0:1, :], waits=[("act", a_bc2)])
        return _finish(nc, es, P, sems)

    xp_spill = []
    b_free = [[("act", a_bc2), ("pe", P.at("pe"))],
              [("act", a_bc2), ("pe", P.at("pe"))]]
    x_free = [[("pe", pe_p4_end)], [("pe", pe_p4_end)]]
    t_free = [[("pe", pe_p4_end), ("dve", P.at("dve"))],
              [("pe", pe_p4_end), ("dve", P.at("dve"))]]
    xp_free = [ap_all + [("pe", pe_p4_end)],
               ap_all + [("pe", pe_p4_end)]]
    sq3_banks = [UT[0:1, 0:512], CXA[0:1, 0:512], CXB[0:1, 0:512],
                 tpb.ap()[0:1, 0:1024].bitcast(F32)[0:1, 0:512]]
    sqp_free = [[("pe", P.at("pe")), ("act", act_attn_end)],
                [("pe", P.at("pe")), ("act", act_attn_end)]]
    st3_pre = [("act", a_bc2), ("dve", P.at("dve")), ("pe", pe_p4_end)]
    last_sum3 = {}
    for dc in range(NDC):
        i = dc % 2
        v1 = dma("qsp", "sab%d" % i, apb[i], ap_scr[dc * 128:(dc + 1) * 128, :],
                 waits=b_free[i] + ap_all)
        v2 = dma("qsp", "sxf%d" % i, xsb[i], x_in[dc * 128:(dc + 1) * 128, :],
                 waits=x_free[i])
        d1 = dve(lambda e, i=i: e.tensor_tensor(tsc5[i], apb[i], mbt2[:, :],
                 OP.subtract), waits=[("sab%d" % i, v1), ("act", a_bc2)] + t_free[i])
        d2 = dve(lambda e, i=i: e.tensor_tensor(xps5[i], tsc5[i], rbt2[:, :],
                 OP.mult), waits=[("dve", d1)] + xp_free[i])
        d3 = dve(lambda e, i=i: e.tensor_tensor(xps5[i], xps5[i], xsb[i], OP.add),
                 waits=[("dve", d2), ("sxf%d" % i, v2)])
        v3 = dma("qsp", "sxp%d" % i, xp_scr[dc * 128:(dc + 1) * 128, :], xps5[i],
                 waits=[("dve", d3)])
        xp_spill.append(v3)
        for tt in range(NTT):
            wv_ = [("dve", d3)] + (st3_pre if dc == 0 else [])
            last_sum3[tt] = mm(ps7[0:1, tt * TT:(tt + 1) * TT], onescolb[:, :],
                               xps5[i][:, tt * TT:(tt + 1) * TT],
                               start=(dc == 0), stop=(dc == NDC - 1), waits=wv_,
                               inc="pe" if dc == NDC - 1 else None)
            sp = tt % 2
            pp = P.op("pool", lambda e, s=sqs5[sp], i=i, t=tt: e.tensor_tensor(
                s, xps5[i][:, t * TT:(t + 1) * TT],
                xps5[i][:, t * TT:(t + 1) * TT], OP.mult),
                waits=[("dve", d3)] + sqp_free[sp], inc="pool")
            pq = mm(sq3_banks[tt], onescolb[:, :], sqs5[sp],
                    start=(dc == 0), stop=(dc == NDC - 1),
                    waits=[("pool", pp)] + (st3_pre if dc == 0 else []),
                    inc="pe")
            sqp_free[sp] = [("pe", pq)]
        b_free[i] = [("dve", d1)]
        x_free[i] = [("dve", d3)]
        t_free[i] = [("dve", d2)]
        xp_free[i] = [("sxp%d" % i, v3), ("pe", P.at("pe"))]

    xp_all = [("sxp0", P.at("sxp0")), ("sxp1", P.at("sxp1"))]
    pe_st3 = P.at("pe")
    s_ev3 = 0
    for tt in range(NTT):
        dve(lambda e, t=tt: e.tensor_copy(rowa[0:1, t * TT:(t + 1) * TT],
            ps7[0:1, t * TT:(t + 1) * TT]), waits=[("pe", pe_st3), rowa_free[0]])
        s_ev3 = dve(lambda e, t=tt: e.tensor_copy(rowb[0:1, t * TT:(t + 1) * TT],
                    sq3_banks[t]), waits=[("pe", pe_st3), rowb_free[0]])
    a_r3 = row_norm_stats(s_ev3)
    a_bc3 = bcast_rows(mbt2, rbt2, a_r3)
    rowa_free[0] = ("act", a_bc3)
    rowb_free[0] = ("act", a_bc3)

    h2_marks = []
    hb_free = [[("act", a_bc3), ("pe", P.at("pe"))],
               [("act", a_bc3), ("pe", P.at("pe"))]]
    t2_free = [[("dve", P.at("dve"))], [("dve", P.at("dve"))]]
    for dc in range(NDC):
        i = dc % 2
        v1 = dma("qsp", "sab%d" % i, apb[i], xp_scr[dc * 128:(dc + 1) * 128, :],
                 waits=hb_free[i] + xp_all)
        wv_ = [("sab%d" % i, v1), ("act", a_bc3), ("pe", pe_p4_end)] + t2_free[i]
        if dc >= 2:
            wv_.append(("dve", h2_marks[dc - 2]))
        d1 = dve(lambda e, i=i: e.tensor_tensor(tsc5[i], apb[i], mbt2[:, :],
                 OP.subtract), waits=wv_)
        d2 = dve(lambda e, i=i, d=dc: e.tensor_tensor(xh(d), tsc5[i], rbt2[:, :],
                 OP.mult), waits=[("dve", d1)])
        h2_marks.append(d2)
        hb_free[i] = [("dve", d1)]
        t2_free[i] = [("dve", d2)]
    h2_all = h2_marks[-1]

    if phase_limit == "h2":
        for dc in range(NDC):
            dma("qgp", "sdbg", out[dc * 128:(dc + 1) * 128, :], xh(dc),
                waits=[("dve", h2_all)])
        return _finish(nc, es, P, sems)

    P.tag = "p6"
    o6_w = o_wbuf
    o6_wdc = o_wbuf + 32768
    o6_g = o6_wdc + 22016
    o6_u = o6_g + 2048
    o6_m = o6_u + 2048
    assert o6_m + 8192 <= AB
    wb6 = [av(o6_w + i * 8192, 128, 4096) for i in range(4)]
    wdc = av(o6_wdc, 128, FF)
    gst = [av(o6_g, 128, TT), av(o6_g + 1024, 128, TT)]
    ust = [av(o6_u, 128, TT), av(o6_u + 1024, 128, TT)]
    mst = [av(o6_m, 128, S), av(o6_m + 4096, 128, S)]

    wb6_free = [[("pe", P.at("pe")), ("dve", P.at("dve"))] for _ in range(4)]
    m_spills = []
    mst_free = [[("dve", P.at("dve")), ("pe", P.at("pe"))],
                [("dve", P.at("dve")), ("pe", P.at("pe"))]]
    g_free = [[("dve", P.at("dve")), ("pe", P.at("pe"))],
              [("dve", P.at("dve")), ("pe", P.at("pe"))]]
    u_free = [[("dve", P.at("dve")), ("pe", P.at("pe"))],
              [("dve", P.at("dve")), ("pe", P.at("pe"))]]
    acc_free6 = [[("dve", s_ev3)], [("dve", s_ev3)]]
    wdc_q = [0]
    wd_conv = []
    m_dst = m_scr.rearrange("t p j -> p t j")

    for fb in range(NFC):
        P.tag = "p6_f%d" % fb
        s0 = (fb % 2) * 2
        sg = ("sg0", "sg1")[fb % 2]
        su = ("su0", "su1")[fb % 2]
        v_g = dma("qgp", sg, wb6[s0], wg_in[fb * 128:(fb + 1) * 128, :],
                  waits=wb6_free[s0] + ([("dve", h2_all)] if fb == 0 else []))
        v_u = dma("qgp", su, wb6[s0 + 1], wu_in[fb * 128:(fb + 1) * 128, :],
                  waits=wb6_free[s0 + 1])
        while len(wd_conv) * NFC <= fb * NDC and len(wd_conv) < NDC:
            db = len(wd_conv)
            vv = dma("qgp", "scv", wdc, wd_in[db * 128:(db + 1) * 128, :],
                     waits=[("scw", wdc_q[0])] + ([("pe", pe_p4_end),
                     ("dve", P.at("dve"))] if db == 0 else []))
            vw = dma("qsp", "scw", wd_scr[db * 128:(db + 1) * 128, :], wdc,
                     waits=[("scv", vv)])
            wdc_q[0] = vw
            wd_conv.append(vw)
        gp = None
        up = None
        for tt in range(NTT):
            i = tt % 2
            for kc in range(NDC):
                wv_ = [(sg, v_g), ("dve", h2_marks[kc])]
                if kc == 0:
                    wv_ = wv_ + acc_free6[0]
                gp = mm(CXA, wb6[s0][:, kc * 128:(kc + 1) * 128],
                        xh(kc)[:, tt * TT:(tt + 1) * TT], start=(kc == 0),
                        stop=(kc == NDC - 1), waits=wv_,
                        inc="pe" if kc == NDC - 1 else None)
            a_g = act(lambda e, i=i: e.activation(gst[i], CXA, AF.Silu, bias=0.0,
                      scale=1.0), waits=[("pe", gp)] + g_free[i])
            acc_free6[0] = [("act", a_g)]
            for kc in range(NDC):
                wv_ = [(su, v_u), ("dve", h2_marks[kc])]
                if kc == 0:
                    wv_ = wv_ + acc_free6[1]
                up = mm(CXB, wb6[s0 + 1][:, kc * 128:(kc + 1) * 128],
                        xh(kc)[:, tt * TT:(tt + 1) * TT], start=(kc == 0),
                        stop=(kc == NDC - 1), waits=wv_,
                        inc="pe" if kc == NDC - 1 else None)
            d_u = dve(lambda e, i=i: e.tensor_copy(ust[i], CXB),
                      waits=[("pe", up)] + u_free[i])
            acc_free6[1] = [("dve", d_u)]
            d_m = dve(lambda e, i=i, t=tt, f=(fb % 2): e.tensor_tensor(
                mst[f][:, t * TT:(t + 1) * TT], gst[i], ust[i], OP.mult),
                waits=[("act", a_g), ("dve", d_u)] + mst_free[fb % 2])
            g_free[i] = [("dve", d_m)]
            u_free[i] = [("dve", d_m)]
        wb6_free[s0] = [("pe", gp)]
        wb6_free[s0 + 1] = [("pe", up)]
        v_m = dma("qsp", ("sm0", "sm1")[fb % 2],
                  m_dst[:, :, fb * TT:(fb + 1) * TT], mst[fb % 2],
                  waits=[("dve", P.at("dve"))])
        m_spills.append(v_m)
        mst_free[fb % 2] = [(("sm0", "sm1")[fb % 2], v_m), ("dve", P.at("dve"))]
    pe_p6_end = P.at("pe")
    m_all = [("sm0", P.at("sm0")), ("sm1", P.at("sm1"))]
    wd_all = [("scw", P.at("scw"))]

    P.tag = "p7"
    o7_mtt = 0
    o7_wd = 90112
    o7_ytt = 134144
    o7_xps = 166912
    o7_outs = 169984
    o7_sq = 174080
    o7_mb = 177152
    assert o7_mb + 6144 <= o_rowa
    mtt = av(o7_mtt, 128, NFC * TT)
    wd7 = [av(o7_wd, 128, FF), av(o7_wd + 22016, 128, FF)]
    ytt = av(o7_ytt, 128, NDC * TT)
    xps7 = [av(o7_xps, 128, TT), av(o7_xps + 1024, 128, TT)]
    outs7 = [av(o7_outs, 128, TT, F32), av(o7_outs + 2048, 128, TT, F32)]
    sq7 = [av(o7_sq, 128, TT), av(o7_sq + 1024, 128, TT)]
    esq7 = [av(o7_mb + 4096, 128, TT), av(o7_mb + 5120, 128, TT)]
    m4bs = [av(o7_mb, 128, TT, F32), av(o_rowa + 1024, 128, TT, F32)]
    r4bs = [av(o7_mb + 2048, 128, TT, F32), av(o_rowb + 1024, 128, TT, F32)]

    dve_p7s = P.at("dve")
    wd7_free = [[("pe", pe_p6_end)], [("pe", pe_p6_end)]]
    mtt_q = [0]
    ytt_free = [[("pe", pe_p6_end)] for _ in range(NDC)]
    xps7_free = [[("scw", P.at("scw")), ("dve", dve_p7s)],
                 [("scw", P.at("scw")), ("dve", dve_p7s)]]
    outs7_free = [[("scw", P.at("scw")), ("dve", dve_p7s)],
                  [("scw", P.at("scw")), ("dve", dve_p7s)]]
    ut_free7 = [P.at("act")]
    rows_free7 = [P.at("dve")]
    sq7_init = [("scw", P.at("scw")), ("dve", dve_p7s)]

    mb_free = {0: [], 1: []}
    epi_state = [None]
    epi_marks = {}
    epi_ix = [0]

    def plan_epilogue_chunk():
        if epi_state[0] is None:
            return
        (ett, y_marks_e, a_m4e, a_r4e, m4be, r4be) = epi_state[0]
        dc = epi_ix[0]
        if dc >= NDC:
            return
        epi_ix[0] += 1
        i = dc % 2
        v_x = dma("qsp", ("sx70", "sx71")[i], xps7[i],
                  xp_scr[dc * 128:(dc + 1) * 128, ett * TT:(ett + 1) * TT],
                  waits=xps7_free[i] + xp_all)
        dd1 = P.op("pool", lambda e, i=i, d=dc, m=m4be: e.tensor_tensor(
            esq7[i], ytt[:, d * TT:(d + 1) * TT], m, OP.subtract),
            waits=[("act", a_m4e), ("dve", y_marks_e[dc])] + esq_free[i],
            inc="pool")
        dd2 = P.op("pool", lambda e, i=i, r=r4be: e.tensor_tensor(
            esq7[i], esq7[i], r, OP.mult),
            waits=[("pool", dd1), ("act", a_r4e)], inc="pool")
        dd3 = P.op("pool", lambda e, i=i: e.tensor_tensor(
            outs7[i], esq7[i], xps7[i], OP.add),
            waits=[("pool", dd2), (("sx70", "sx71")[i], v_x)] + outs7_free[i],
            inc="pool")
        v_o = dma("qsp", ("so0", "so1")[i],
                  out[dc * 128:(dc + 1) * 128, ett * TT:(ett + 1) * TT],
                  outs7[i], waits=[("pool", dd3)])
        xps7_free[i] = [("pool", dd3)]
        esq_free[i] = [("pool", dd3)]
        outs7_free[i] = [(("so0", "so1")[i], v_o)]
        epi_marks[(ett, dc)] = dd1

    esq_free = [[("scw", P.at("scw"))], [("scw", P.at("scw"))]]

    for tt in range(NTT):
        P.tag = "p7_t%d" % tt
        v_m = dma("qgp", "smt", mtt[:, :], m_scr[tt, :, :],
                  waits=m_all + [("pe", pe_p6_end if tt == 0 else mtt_q[0])])
        sum_ps = ps7[0:1, 2048:2560]
        sq_ps = ps7[0:1, 0:512]
        y_marks = {}
        sq_free7 = [list(sq7_init), list(sq7_init)]
        stats_last = None
        pv = None
        m4b = m4bs[tt % 2]
        r4b = r4bs[tt % 2]
        for db in range(NDC):
            plan_epilogue_chunk()
            sl = db % 2
            v_w = dma("qgp", ("swd0", "swd1")[sl],
                      wd7[sl], wd_scr[db * 128:(db + 1) * 128, :],
                      waits=wd7_free[sl] + wd_all)
            bi = db % 2
            bank = [CXA, CXB][bi]
            for fc in range(NFC):
                wv_ = [(("swd0", "swd1")[sl], v_w), ("smt", v_m)]
                if fc == 0:
                    wv_ = wv_ + acc_free6[bi]
                pv = mm(bank, wd7[sl][:, fc * 128:(fc + 1) * 128],
                        mtt[:, fc * TT:(fc + 1) * TT], start=(fc == 0),
                        stop=(fc == NFC - 1), waits=wv_,
                        inc="pe" if fc == NFC - 1 else None)
            wd7_free[sl] = [("pe", pv)]
            yw = [("pe", pv)] + ytt_free[db]
            if tt > 0:
                yw.append(("pool", epi_marks[(tt - 1, db)]))
            d_y = dve(lambda e, b=bank, d=db: e.tensor_copy(
                ytt[:, d * TT:(d + 1) * TT], b), waits=yw)
            acc_free6[bi] = [("dve", d_y)]
            y_marks[db] = d_y
            mm(sum_ps, onescolb[:, :], ytt[:, db * TT:(db + 1) * TT],
               start=(db == 0), stop=(db == NDC - 1),
               waits=[("dve", d_y)] + ([("act", ut_free7[0]),
               ("dve", rows_free7[0])] if db == 0 else []),
               inc="pe" if db == NDC - 1 else None)
            dd = dve(lambda e, s=sq7[db % 2], d=db: e.tensor_tensor(
                s, ytt[:, d * TT:(d + 1) * TT], ytt[:, d * TT:(d + 1) * TT],
                OP.mult), waits=[("dve", d_y)] + sq_free7[db % 2])
            pv2 = mm(sq_ps, onescolb[:, :], sq7[db % 2], start=(db == 0),
                     stop=(db == NDC - 1),
                     waits=[("dve", dd)] + (reg_free[0] if db == 0 and tt == 0
                                            else []),
                     inc="pe")
            sq_free7[db % 2] = [("pe", pv2)]
            stats_last = P.at("pe")
        mtt_q[0] = pv
        d1 = dve(lambda e: e.tensor_copy(rowa[0:1, 0:TT], ps7[0:1, 2048:2560]),
                 waits=[("pe", stats_last), rowa_free[0]])
        d2 = dve(lambda e: e.tensor_copy(rowb[0:1, 0:TT], ps7[0:1, 0:512]),
                 waits=[("pe", stats_last), rowb_free[0]])
        rowa_free[0] = ("dve", d1)
        a_r4r = row_norm_stats(max(d1, d2), n=TT,
                               m2_waits=[("dve", rows_free7[0])])
        pvb = mm(UT, negb[0:1, 0:128], rowa[0:1, 0:TT], start=True, stop=True,
                 waits=[("act", a_r4r), ("act", ut_free7[0]), ("pool", c_mem)],
                 inc="pe")
        a_m4 = act(lambda e, m=m4b: e.activation(m[:, :], UT[:, 0:TT], AF.Copy,
                   bias=0.0, scale=-1.0), waits=[("pe", pvb)] + mb_free[tt % 2])
        pvb2 = mm(UT, negb[0:1, 0:128], rowb[0:1, 0:TT], start=True, stop=True,
                  waits=[("act", a_m4)], inc="pe")
        a_r4 = act(lambda e, r=r4b: e.activation(r[:, :], UT[:, 0:TT], AF.Copy,
                   bias=0.0, scale=-1.0), waits=[("pe", pvb2)] + mb_free[tt % 2])
        ut_free7[0] = a_r4
        rows_free7[0] = P.at("dve")
        rowa_free[0] = ("pe", pvb)
        rowb_free[0] = ("pe", pvb2)
        while epi_state[0] is not None and epi_ix[0] < NDC:
            plan_epilogue_chunk()
        epi_state[0] = (tt, dict(y_marks), a_m4, a_r4, m4b, r4b)
        epi_ix[0] = 0
        mb_free[tt % 2] = []
    while epi_ix[0] < NDC:
        plan_epilogue_chunk()

    return _finish(nc, es, P, sems)


def _finish(nc, es, P, sems):
    with nc.Block() as block:
        @block.tensor
        def _(te):
            P.emit("pe", te, sems)

        @block.scalar
        def _(sc):
            P.emit("act", sc, sems)

        @block.vector
        def _(v):
            P.emit("dve", v, sems)

        @block.gpsimd
        def _(g):
            P.emit("pool", g, sems)

        @block.sync
        def _(sy):
            P.emit("sp", sy, sems)
            for s in P.cnt:
                if s not in ("pe", "act", "dve", "pool") and P.at(s) > 0:
                    sy.wait_ge(sems[s], P.at(s))
    es.close()
    return nc


def kernel(**inputs):
    in_maps = _host_layout(inputs)
    nc = build_nc()
    res = run_bass_kernel_spmd(nc, in_maps, core_ids=list(range(8)))
    outs = []
    for c in range(8):
        outs.append(np.ascontiguousarray(res.results[c]["out"].T))
    return np.stack(outs, 0)


# version 06
# speedup vs baseline: 1.2177x; 1.2177x over previous
import contextlib
import numpy as np
import ml_dtypes
import concourse.bass as bass
import concourse.mybir as mybir
from concourse.bass_utils import run_bass_kernel_spmd

F32 = mybir.dt.float32
BF16 = mybir.dt.bfloat16
AF = mybir.ActivationFunctionType
AX = mybir.AxisListType
OP = mybir.AluOpType

S = 2048
DM = 4096
NKV = 8
RQ = 4
FF = 11008
NDC = DM // 128
NFC = FF // 128
NTT = 4
TT = 512
NQT = S // 128
EPS = 1e-5
QSCALE = float(1.0 / np.sqrt(128.0))
EXPC = 10.0


class Prog:
    def __init__(self, sems):
        self.recs = []
        self.cnt = {k: 0 for k in sems}
        self.tag = ""

    def op(self, eng, fn, waits=(), inc=None, inc_n=1):
        ws = tuple((s, v) for (s, v) in waits if v and v > 0)
        self.recs.append((eng, ws, fn, inc, inc_n, self.tag))
        if inc is not None:
            self.cnt[inc] += inc_n
            return self.cnt[inc]
        return None

    def at(self, sem):
        return self.cnt[sem]

    def emit(self, eng_name, eng, sems):
        waited = {}
        for (e, ws, fn, inc, inc_n, tag) in self.recs:
            if e != eng_name:
                continue
            for (s, v) in ws:
                if waited.get(s, 0) < v:
                    eng.wait_ge(sems[s], v)
                    waited[s] = v
            inst = fn(eng)
            if tag:
                inst.annotate(tag)
            if inc is not None:
                inst.then_inc(sems[inc], inc_n)


def _host_layout(inputs):
    x = np.asarray(inputs["L0_attn_hidden_states"], dtype=np.float32)

    def proj_slabs(w, nob):
        a = np.asarray(w, dtype=np.float32).reshape(NDC, 128, nob, 128)
        return np.ascontiguousarray(a.transpose(2, 1, 0, 3)).reshape(nob * 128, NDC * 128)

    def ffn_in_slabs(w):
        a = np.asarray(w, dtype=np.float32).reshape(NFC, 128, NDC, 128).transpose(0, 3, 2, 1)
        return np.ascontiguousarray(a).reshape(NFC * 128, NDC * 128)

    def ffn_out_slabs(w):
        a = np.asarray(w, dtype=np.float32).reshape(NDC, 128, NFC, 128).transpose(0, 3, 2, 1)
        return np.ascontiguousarray(a).reshape(NDC * 128, NFC * 128)

    shared = {
        "wq": proj_slabs(inputs["L0_attn_Wq"], 32),
        "wk": proj_slabs(inputs["L0_attn_Wk"], 8),
        "wv": proj_slabs(inputs["L0_attn_Wv"], 8),
        "wo": proj_slabs(inputs["L0_attn_Wo"], 32),
        "wg": ffn_in_slabs(inputs["L0_ffn_W_gate"]),
        "wu": ffn_in_slabs(inputs["L0_ffn_W_up"]),
        "wd": ffn_out_slabs(inputs["L0_ffn_W_down"]),
        "bq": np.ascontiguousarray(inputs["L0_attn_bq"], dtype=np.float32).reshape(1, DM),
        "bk": np.ascontiguousarray(inputs["L0_attn_bk"], dtype=np.float32).reshape(1, 1024),
        "bv": np.ascontiguousarray(inputs["L0_attn_bv"], dtype=np.float32).reshape(1, 1024),
        "bo": np.ascontiguousarray(inputs["L0_attn_bo"], dtype=np.float32).reshape(1, DM),
    }
    in_maps = []
    for c in range(8):
        m = dict(shared)
        m["x"] = np.ascontiguousarray(x[c].T)
        in_maps.append(m)
    return in_maps


def _consts():
    half = 64
    inv = 1.0 / (10000.0 ** (np.arange(half, dtype=np.float64) / half))
    ang = np.arange(S, dtype=np.float64)[None, :] * inv[:, None]
    cos = np.cos(ang).astype(np.float32)
    sin = np.sin(ang).astype(np.float32)
    cos_d = np.concatenate([cos, cos], 0).astype(ml_dtypes.bfloat16)
    sin_d = np.concatenate([sin, sin], 0).astype(ml_dtypes.bfloat16)
    ident = np.eye(128, dtype=ml_dtypes.bfloat16)
    q_ = np.arange(128)[:, None]
    k_ = np.arange(128)[None, :]
    tri = np.where(k_ <= q_, 0.0, -1e30).astype(np.float32)
    return cos_d, sin_d, ident, tri


def build_nc(phase_limit=None):
    nc = bass.Bass("TRN2", target_bir_lowering=False)
    es = contextlib.ExitStack()

    x_in = nc.dram_tensor("x", [DM, S], F32, kind="ExternalInput").ap()
    wq_in = nc.dram_tensor("wq", [32 * 128, NDC * 128], F32, kind="ExternalInput").ap()
    wk_in = nc.dram_tensor("wk", [8 * 128, NDC * 128], F32, kind="ExternalInput").ap()
    wv_in = nc.dram_tensor("wv", [8 * 128, NDC * 128], F32, kind="ExternalInput").ap()
    wo_in = nc.dram_tensor("wo", [32 * 128, NDC * 128], F32, kind="ExternalInput").ap()
    wg_in = nc.dram_tensor("wg", [NFC * 128, NDC * 128], F32, kind="ExternalInput").ap()
    wu_in = nc.dram_tensor("wu", [NFC * 128, NDC * 128], F32, kind="ExternalInput").ap()
    wd_in = nc.dram_tensor("wd", [NDC * 128, NFC * 128], F32, kind="ExternalInput").ap()
    bq_in = nc.dram_tensor("bq", [1, DM], F32, kind="ExternalInput").ap()
    bk_in = nc.dram_tensor("bk", [1, 1024], F32, kind="ExternalInput").ap()
    bv_in = nc.dram_tensor("bv", [1, 1024], F32, kind="ExternalInput").ap()
    bo_in = nc.dram_tensor("bo", [1, DM], F32, kind="ExternalInput").ap()
    out = nc.dram_tensor("out", [DM, S], F32, kind="ExternalOutput").ap()

    ctx_scr = nc.dram_tensor("ctx_scr", [NDC * 128, S], BF16, kind="Internal").ap()
    ap_scr = nc.dram_tensor("ap_scr", [NDC * 128, S], BF16, kind="Internal").ap()
    xp_scr = nc.dram_tensor("xp_scr", [NDC * 128, S], BF16, kind="Internal").ap()
    m_scr = nc.dram_tensor("m_scr", [NTT, 128, NFC * TT], BF16, kind="Internal").ap()
    wd_scr = nc.dram_tensor("wd_scr", [NDC * 128, NFC * 128], BF16, kind="Internal").ap()

    cos_np, sin_np, ident_np, tri_np = _consts()
    cos_t = nc.inline_tensor(cos_np, name="cos_t")
    sin_t = nc.inline_tensor(sin_np, name="sin_t")
    ident_t = nc.inline_tensor(ident_np, name="ident_t")
    tri_t = nc.inline_tensor(tri_np, name="tri_t")
    trib16_t = nc.inline_tensor(tri_np.astype(ml_dtypes.bfloat16), name="trib16_t")

    identb = es.enter_context(nc.sbuf_tensor("identb", [128, 128], BF16))
    cosb = es.enter_context(nc.sbuf_tensor("cosb", [128, S], BF16))
    sinb = es.enter_context(nc.sbuf_tensor("sinb", [128, S], BF16))
    trib = es.enter_context(nc.sbuf_tensor("trib", [128, 128], F32))
    trib16 = es.enter_context(nc.sbuf_tensor("trib16", [128, 128], BF16))
    onesr = es.enter_context(nc.sbuf_tensor("onesr", [1, TT], BF16))
    negb = es.enter_context(nc.sbuf_tensor("negb", [1, 128], BF16))
    onescolb = es.enter_context(nc.sbuf_tensor("onescolb", [128, 1], BF16))

    ar_bytes = nc.sbuf_bytes_remaining - 128
    ar_elems = (ar_bytes // 512) * 256
    arena = es.enter_context(nc.sbuf_tensor("arena", [128, ar_elems], BF16))
    AB = ar_elems * 2

    def av(off, part, elems, dtype=BF16):
        assert off % 4 == 0
        if dtype == BF16:
            return arena[0:part, off // 2: off // 2 + elems]
        assert dtype == F32
        return arena[0:part, off // 2: off // 2 + 2 * elems].bitcast(F32)

    ps7 = es.enter_context(nc.psum_tensor("ps7", [128, 3584], F32))
    tpb = es.enter_context(nc.psum_tensor("tpb", [128, 1024], BF16))
    R0 = ps7[:, 0:1024]
    R1 = ps7[:, 1024:2048]
    UT = ps7[:, 2048:2560]
    CXA = ps7[:, 2560:3072]
    CXB = ps7[:, 3072:3584]

    dma_sems = ["scst", "sx", "swb0", "swb1", "sbias", "sctx", "scxr",
                "sap0", "sap1", "sab0", "sab1", "sxf0", "sxf1", "sxp0", "sxp1",
                "sg0", "sg1", "su0", "su1", "scv", "scw", "sm0", "sm1",
                "smt", "swd0", "swd1", "sx70", "sx71", "so0", "so1", "sdbg"]
    sem_names = ["pe", "act", "dve", "pool"] + dma_sems
    sems = {s: es.enter_context(nc.semaphore(s)) for s in sem_names}
    P = Prog(sem_names)

    def dma(queue, sem, dst, src, waits=()):
        eng = "pool" if queue == "qgp" else "sp"
        return P.op(eng, lambda e, d=dst, s=src: e.dma_start(d, s), waits=waits,
                    inc=sem, inc_n=16)

    def mm(dst, lhsT, rhs, start, stop, waits=(), inc=None):
        return P.op("pe", lambda e, d=dst, l=lhsT, r=rhs, st=start, sp=stop:
                    e.matmul(d, l, r, start=st, stop=sp), waits=waits, inc=inc)

    def dve(fn, waits=()):
        return P.op("dve", fn, waits=waits, inc="dve")

    def act(fn, waits=()):
        return P.op("act", fn, waits=waits, inc="act")

    dma("qsp", "scst", identb[:, :], ident_t.ap())
    dma("qsp", "scst", cosb[:, :], cos_t.ap())
    dma("qsp", "scst", sinb[:, :], sin_t.ap())
    dma("qsp", "scst", trib[:, :], tri_t.ap())
    dma("qsp", "scst", trib16[:, :], trib16_t.ap())
    c_const = ("scst", P.at("scst"))
    P.op("pool", lambda e: e.memset(onesr[:, :], 1.0), inc="pool")
    P.op("pool", lambda e: e.memset(negb[:, :], -1.0), inc="pool")
    P.op("pool", lambda e: e.memset(onescolb[:, :], 1.0), inc="pool")
    c_mem = P.at("pool")

    o_xh = 0
    o_wbuf = 131072
    o_kvq = 147456
    o_psb = 163840
    o_ptsb = 172032
    o_ctxh = 176128
    o_ropeA = 180224
    o_ropeB = 182272
    o_rowa = 184320
    o_rowb = 188416
    o_nbc = 192512
    o_rs = o_nbc + 64
    o_rc = o_nbc + 96
    o_kmaxf = o_nbc + 112
    o_rs5 = o_nbc + 116
    o_bkr = o_nbc + 160
    o_bvr = o_nbc + 416
    o_bqr = o_nbc + 672
    o_m2 = o_nbc + 1696
    o_rcb = o_m2 + 4160
    assert o_rcb + 2048 <= AB, f"arena {AB} < {o_rcb + 2048}"

    def xh(dc):
        return arena[:, (o_xh // 2) + dc * S:(o_xh // 2) + (dc + 1) * S]

    rowa = av(o_rowa, 1, S)
    rowb = av(o_rowb, 1, S)
    m2row = av(o_m2, 1, S)
    kmaxf = av(o_kmaxf, 1, 1, F32)

    P.tag = "p1"
    mbt = av(o_wbuf, 128, S)
    rbt = av(o_wbuf + 4096, 128, S)
    sqsc = [av(o_kvq + i * 1024, 128, TT) for i in range(2)]

    for dc in range(NDC):
        dma("qgp", "sx", xh(dc), x_in[dc * 128:(dc + 1) * 128, :])
    x_all = P.at("sx")

    st_ev = 0
    sq_slot_pe = [0, 0]
    for tt in range(NTT):
        sum_ps = ps7[0:1, 0:512]
        sq_ps = ps7[0:1, 512:1024]
        for dc in range(NDC):
            wv_ = [("sx", x_all), ("pool", c_mem), ("dve", st_ev)]
            mm(sum_ps, onescolb[:, :], xh(dc)[:, tt * TT:(tt + 1) * TT],
               start=(dc == 0), stop=(dc == NDC - 1), waits=wv_,
               inc="pe" if dc == NDC - 1 else None)
            sl = sqsc[dc % 2]
            dv = dve(lambda e, s=sl, d=dc, t=tt: e.tensor_tensor(
                s, xh(d)[:, t * TT:(t + 1) * TT], xh(d)[:, t * TT:(t + 1) * TT],
                OP.mult), waits=[("sx", x_all), ("pe", sq_slot_pe[dc % 2])])
            pv = mm(sq_ps, onescolb[:, :], sl, start=(dc == 0), stop=(dc == NDC - 1),
                    waits=[("dve", dv), ("dve", st_ev)], inc="pe")
            sq_slot_pe[dc % 2] = pv
        pe_tt = P.at("pe")
        d1 = dve(lambda e, t=tt: e.tensor_copy(rowa[0:1, t * TT:(t + 1) * TT],
                 ps7[0:1, 0:512]), waits=[("pe", pe_tt)])
        d2 = dve(lambda e, t=tt: e.tensor_copy(rowb[0:1, t * TT:(t + 1) * TT],
                 ps7[0:1, 512:1024]), waits=[("pe", pe_tt)])
        st_ev = d2

    def row_norm_stats(dep_dve, n=S, m2_waits=()):
        d_m = dve(lambda e: e.tensor_scalar(rowa[0:1, 0:n], rowa[0:1, 0:n],
                  1.0 / DM, None, OP.mult), waits=[("dve", dep_dve)])
        d_m2 = dve(lambda e: e.tensor_tensor(m2row[0:1, 0:n], rowa[0:1, 0:n],
                   rowa[0:1, 0:n], OP.mult), waits=[("dve", d_m)] + list(m2_waits))
        d_v = dve(lambda e: e.scalar_tensor_tensor(rowb[0:1, 0:n], rowb[0:1, 0:n],
                  1.0 / DM, m2row[0:1, 0:n], OP.mult, OP.subtract),
                  waits=[("dve", d_m2), ("dve", dep_dve)])
        d_ve = dve(lambda e: e.tensor_scalar(rowb[0:1, 0:n], rowb[0:1, 0:n],
                   EPS, None, OP.add), waits=[("dve", d_v)])
        a_l = act(lambda e: e.activation(rowb[0:1, 0:n], rowb[0:1, 0:n], AF.Ln,
                  bias=0.0, scale=1.0), waits=[("dve", d_ve)])
        a_e = act(lambda e: e.activation(rowb[0:1, 0:n], rowb[0:1, 0:n], AF.Exp,
                  bias=0.0, scale=-0.5), waits=[("act", a_l)])
        return a_e

    a_rstd = row_norm_stats(st_ev)

    def bcast_rows(mdst, rdst, a_dep, blocks=NTT, bw=TT, extra=()):
        ut_free = P.at("act")
        last_a = None
        for (row, dst) in ((rowa, mdst), (rowb, rdst)):
            for b in range(blocks):
                pv = mm(UT, negb[0:1, 0:128], row[0:1, b * bw:(b + 1) * bw],
                        start=True, stop=True,
                        waits=[("act", a_dep), ("act", ut_free), ("pool", c_mem)],
                        inc="pe")
                last_a = act(lambda e, b=b, d=dst, w=bw: e.activation(
                    d[:, b * w:(b + 1) * w], UT[:, 0:w], AF.Copy, bias=0.0,
                    scale=-1.0), waits=[("pe", pv)] + list(extra))
                ut_free = last_a
        return last_a

    a_bc1 = bcast_rows(mbt, rbt, a_rstd)

    tscr = [av(o_psb, 128, S), av(o_psb + 4096, 128, S)]
    h_marks = []
    for dc in range(NDC):
        sl = tscr[dc % 2]
        wv_ = [("act", a_bc1)]
        if dc >= 2:
            wv_.append(("dve", h_marks[dc - 2]))
        d1 = dve(lambda e, d=dc, s=sl: e.tensor_tensor(s, xh(d), mbt[:, :],
                 OP.subtract), waits=wv_)
        d2 = dve(lambda e, d=dc, s=sl: e.tensor_tensor(xh(d), s, rbt[:, :], OP.mult),
                 waits=[("dve", d1)])
        h_marks.append(d2)
    h_all = h_marks[-1]

    if phase_limit == "p1":
        for dc in range(NDC):
            dma("qgp", "sdbg", out[dc * 128:(dc + 1) * 128, :], xh(dc),
                waits=[("dve", h_all)])
        return _finish(nc, es, P, sems)

    P.tag = "p23setup"
    wbuf = [av(o_wbuf, 128, 4096), av(o_wbuf + 8192, 128, 4096)]
    kK = av(o_kvq, 128, S)
    kV = av(o_kvq + 4096, 128, S)
    kQ = [av(o_kvq + 8192, 128, S), av(o_kvq + 12288, 128, S)]
    psb = [av(o_psb, 128, S), av(o_psb + 4096, 128, S)]
    psbh = [av(o_psb + j * 2048, 128, 1024) for j in range(4)]
    pth = [av(o_ptsb + j * 2048, 128, 1024) for j in range(2)]
    ctxh = av(o_ctxh, 128, S)
    ropeA = av(o_ropeA, 128, 1024)
    ropeB = av(o_ropeB, 128, 1024)
    nbc = av(o_nbc, 128, 16, F32)
    rs4 = av(o_rs, 128, 8, F32)
    rsb4 = av(o_rc, 128, 8)
    rs5s = [av(o_rs5, 128, 1, F32), av(o_rs5 + 4, 128, 1, F32)]
    rcb = av(o_rcb, 128, 512, F32)
    rrow = m2row
    bkr = av(o_bkr, 1, 128)
    bvr = av(o_bvr, 1, 128)
    bqr = av(o_bqr, 1, 512)

    wb_free = [[("dve", h_all)], [("dve", h_all)]]
    wb_loaded = [0, 0]
    slot_turn = [0]
    acc_free = [[("dve", 0)], [("dve", 0)]]
    reg_free = [[("dve", 0)], [("dve", 0)]]
    reg_turn = [0]
    tp_free = [("act", 0)]
    pt_free = [("pe", 0)]
    pbuf_free = [[("dve", h_all)], [("dve", h_all)]]
    ctxh_free = [("qsp", 0)]
    nbc_free = [("act", 0)]
    rowa_free = [("act", a_bc1)]
    rowb_free = [("act", a_bc1)]
    scrA_free = [("dve", 0)]
    scrB_free = [("dve", 0)]
    kmax_mark = [0]
    kv_mark = [0]
    ctx_evac = {0: ("dve", 0), 1: ("dve", 0)}
    q_slot_attn_pe = [0, 0]
    rs_cast = [0, 0]
    rs_done = {0: 0, 1: 0, 2: 0, 3: 0}
    pring = [0]
    ptring = [0]
    ph_free = [[("dve", h_all)] for _ in range(4)]
    ptf_free = {0: ("pe", 0), 1: ("pe", 0)}
    ut_mm = [0]
    ut_free_a = [("act", 0)]
    rrow_free = [("pe", 0)]
    rcb_free = [("dve", 0)]

    def load_slab(src_ap):
        sl = slot_turn[0] % 2
        slot_turn[0] += 1
        sname = "swb%d" % sl
        v = dma("qgp", sname, wbuf[sl][:, :], src_ap, waits=wb_free[sl])
        wb_loaded[sl] = (sname, v)
        return sl

    acc_turn = [0]

    def proj_chunk(sl, bias_row, evac_fn, extra=()):
        last_pv = None
        for tt in range(NTT):
            bi = acc_turn[0] % 2
            acc_turn[0] += 1
            bank = [CXA, CXB][bi]
            for kc in range(NDC):
                wv_ = [wb_loaded[sl], ("dve", h_marks[kc])] + list(extra)
                if kc == 0:
                    wv_ = wv_ + acc_free[bi]
                mm(bank, wbuf[sl][:, kc * 128:(kc + 1) * 128],
                   xh(kc)[:, tt * TT:(tt + 1) * TT], start=(kc == 0), stop=False,
                   waits=wv_)
            pv = mm(bank, bias_row, onesr[0:1, 0:TT], start=False, stop=True,
                    waits=[("pool", c_mem)], inc="pe")
            mk = evac_fn(bank, pv, tt)
            acc_free[bi] = [("dve", mk)]
            last_pv = pv
        return last_pv

    def rope_inplace(dst, pre_waits=(), tt_ready=None):
        marks = []
        for hf in range(2):
            cs = slice(hf * 1024, hf * 1024 + 1024)
            wv0 = scrA_free + scrB_free + list(pre_waits) + [c_const]
            if tt_ready is not None:
                wv0 = wv0 + [tt_ready[2 * hf], tt_ready[2 * hf + 1]]
            dve(lambda e, cs=cs: e.tensor_tensor(ropeA[0:64, :], dst[0:64, cs],
                cosb[0:64, cs], OP.mult), waits=wv0)
            dve(lambda e, cs=cs: e.tensor_tensor(ropeA[64:128, :], dst[0:64, cs],
                sinb[0:64, cs], OP.mult))
            dve(lambda e, cs=cs: e.tensor_tensor(ropeB[0:64, :], dst[64:128, cs],
                sinb[64:128, cs], OP.mult))
            e4 = dve(lambda e, cs=cs: e.tensor_tensor(ropeB[64:128, :],
                     dst[64:128, cs], cosb[64:128, cs], OP.mult))
            dve(lambda e, cs=cs: e.tensor_tensor(dst[0:64, cs], ropeA[0:64, :],
                ropeB[0:64, :], OP.subtract), waits=[("dve", e4)])
            e_last = dve(lambda e, cs=cs: e.tensor_tensor(dst[64:128, cs],
                         ropeA[64:128, :], ropeB[64:128, :], OP.add),
                         waits=[("dve", e4)])
            scrA_free[0] = ("dve", e_last)
            scrB_free[0] = ("dve", e_last)
            marks.append(e_last)
        return marks

    def sq_row_mms(srcbuf, src_marks, extra=()):
        sq_slots = [ropeA[:, 0:TT], ropeB[:, 0:TT]]
        slot_w = [scrA_free, scrB_free]
        out = []
        for tt in range(NTT):
            i = tt % 2
            dd = dve(lambda e, s=sq_slots[i], t=tt: e.tensor_tensor(
                s, srcbuf[:, t * TT:(t + 1) * TT], srcbuf[:, t * TT:(t + 1) * TT],
                OP.mult), waits=[("dve", src_marks[tt])] + slot_w[i])
            pv = mm(ps7[0:1, tt * TT:(tt + 1) * TT], onescolb[:, :], sq_slots[i],
                    start=True, stop=True, waits=[("dve", dd)] + list(extra),
                    inc="pe")
            slot_w[i][0] = ("pe", pv)
            out.append(pv)
        return out

    def attention_head(g, hh, qbuf, q_marks):
        hg = 4 * g + hh
        P.tag = "attn_g%d_h%d" % (g, hh)
        qm = {0: ("dve", q_marks[0]), 1: ("dve", q_marks[0]),
              2: ("dve", q_marks[1]), 3: ("dve", q_marks[1])}
        sq_pe = sq_row_mms(qbuf, {t: qm[t][1] for t in range(4)},
                           extra=reg_free[0] + reg_free[1] + [rowa_free[0]])
        d_ev0 = dve(lambda e: e.tensor_copy(rowa[0:1, 0:1024], ps7[0:1, 0:1024]),
                    waits=[("pe", sq_pe[1]), rowa_free[0]])
        d_ev1 = dve(lambda e: e.tensor_copy(rowa[0:1, 1024:2048],
                    ps7[0:1, 1024:2048]), waits=[("pe", sq_pe[3]), rowa_free[0]])
        d_u1 = dve(lambda e: e.tensor_scalar(rowa[:, :], rowa[:, :],
                   kmaxf[0:1, 0:1], 1e-20, OP.mult, OP.add),
                   waits=[("dve", d_ev1), ("dve", kmax_mark[0])])
        a_u2 = act(lambda e: e.activation(rowa[:, :], rowa[:, :], AF.Ln, bias=0.0,
                   scale=1.0), waits=[("dve", d_u1)])
        a_u3 = act(lambda e: e.activation(rowa[:, :], rowa[:, :], AF.Exp, bias=0.0,
                   scale=0.5), waits=[("act", a_u2)])
        nb_mm = None
        for qt in range(NQT):
            nb_mm = mm(ps7[:, 2048 + qt:2049 + qt],
                       rowa[0:1, qt * 128:(qt + 1) * 128], negb[0:1, 0:1],
                       start=True, stop=True,
                       waits=[("act", a_u3), ("pool", c_mem), nbc_free[0],
                              ut_free_a[0]],
                       inc="pe" if qt == NQT - 1 else None)
        rowa_free[0] = ("pe", nb_mm)
        a_nb = act(lambda e: e.activation(nbc[:, :], ps7[:, 2048:2064], AF.Copy,
                   bias=EXPC, scale=1.0), waits=[("pe", nb_mm)])
        nbc_free[0] = ("act", a_nb)

        halves = []
        for qt in range(NQT):
            nkc = qt + 1
            hs = [(qt, 0, 0, min(nkc, 8))] + ([(qt, 1, 8, nkc)] if nkc > 8 else [])
            halves.extend(hs)
        DELAY = 2
        st = {}

        def stage1(j):
            (qt, hi, kc0, kc1) = halves[j]
            P.tag = "attn_g%d_h%d_q%d" % (g, hh, qt)
            nkc = qt + 1
            gset = (qt // 4) % 2
            ri = gset * 4 + qt % 4
            rs_c = rs4[:, ri:ri + 1]
            rs5 = rs5s[gset]
            rgi = reg_turn[0] % 2
            reg_turn[0] += 1
            reg = [R0, R1][rgi]
            pj = pring[0] % 4
            pring[0] += 1
            ptile = psbh[pj]
            wv_ = (reg_free[rgi] + [qm[min(3, (kc1 - 1) // 4)], ("dve", kv_mark[0]),
                                    ("dve", d_ev0 if kc1 <= 8 else d_ev1), c_const]
                   + [qm[min(3, qt // 4)]])
            has_diag = (kc1 - 1 == qt)
            lastq = None
            nc_cols = (kc1 - kc0) * 128
            for c0 in range(0, nc_cols, 512):
                cw = min(512, nc_cols - c0)
                last_chunk = (c0 + cw == nc_cols)
                lastq = mm(reg[:, c0:c0 + cw],
                           qbuf[:, qt * 128:(qt + 1) * 128],
                           kK[:, kc0 * 128 + c0:kc0 * 128 + c0 + cw],
                           start=True, stop=not (has_diag and last_chunk),
                           waits=wv_,
                           inc=None if (has_diag or not last_chunk) else "pe")
                wv_ = []
            if has_diag:
                w = (qt - kc0) * 128
                lastq = mm(reg[:, w:w + 128], identb[:, :], trib16[:, :],
                           start=False, stop=True, waits=[], inc="pe")
            else:
                lastq = P.at("pe")
            rcol = rs_c if hi == 0 else rs5
            ewait = [("pe", lastq), ("act", a_nb), ("dve", rs_cast[gset])] \
                + ph_free[pj]
            av_ = act(lambda e, r=reg, p=ptile, w=nc_cols, q=qt, rc=rcol:
                      e.activation(p[:, 0:w], r[:, 0:w], AF.Exp,
                      bias=nbc[:, q:q + 1], scale=1.0, accum_out=rc),
                      waits=ewait)
            reg_free[rgi] = [("act", av_)]
            if hi == 1 or qt < 8:
                if hi == 1:
                    d_rs = dve(lambda e, a=rs_c, b=rs5: e.tensor_tensor(
                        a, a, b, OP.add), waits=[("act", av_)])
                else:
                    d_rs = dve(lambda e: e.engine_nop(), waits=[("act", av_)])
                rs_done[qt % 4] = d_rs
            st[j] = dict(qt=qt, hi=hi, kc0=kc0, kc1=kc1, pj=pj, av=av_,
                         ptile=ptile)

        def stage2(j):
            s = st.pop(j)
            qt, hi, kc0, kc1 = s["qt"], s["hi"], s["kc0"], s["kc1"]
            P.tag = "attn_g%d_h%d_q%dT" % (g, hh, qt)
            nkc = qt + 1
            qg = qt // 4
            cb = qg % 2
            bank = [CXA, CXB][cb]
            nc_cols = (kc1 - kc0) * 128
            tj = ptring[0] % 2
            ptring[0] += 1
            pts = pth[tj]
            t_last = None
            for kc in range(kc0, kc1):
                t_last = P.op("pe", lambda e, p=s["ptile"], k=(kc - kc0):
                              e.transpose(tpb[:, k * 128:(k + 1) * 128],
                              p[:, k * 128:(k + 1) * 128], identb[:, :]),
                              waits=[("act", s["av"]), c_const,
                                     tp_free[0], ptf_free[tj]],
                              inc="pe" if kc == kc1 - 1 else None)
            ph_free[s["pj"]] = [("pe", t_last)]
            d_pt = dve(lambda e, n=nc_cols, d=pts: e.tensor_copy(
                d[:, 0:n], tpb[:, 0:n]), waits=[("pe", t_last)])
            tp_free[0] = ("dve", d_pt)
            pv_last = None
            for kc in range(kc0, kc1):
                wv_ = [("dve", d_pt), ("dve", kv_mark[0])]
                if qt % 4 == 0 and hi == 0 and kc == 0:
                    wv_ = wv_ + acc_free[cb] + [ctx_evac[cb]]
                pv_last = mm(bank[:, (qt % 4) * 128:(qt % 4 + 1) * 128],
                             kV[:, kc * 128:(kc + 1) * 128],
                             pts[:, (kc - kc0) * 128:(kc - kc0 + 1) * 128],
                             start=(qt % 4 == 0 and hi == 0 and kc == 0),
                             stop=(qt % 4 == 3 and kc == nkc - 1),
                             waits=wv_, inc="pe" if kc == kc1 - 1 else None)
            ptf_free[tj] = ("pe", pv_last)
            if qt % 4 == 3 and kc1 == nkc:
                gset2 = qg % 2
                d_cast = dve(lambda e, s=gset2: e.tensor_copy(
                    rsb4[:, 0:4], rs4[:, s * 4:s * 4 + 4]),
                    waits=[("dve", rs_done[0]), ("dve", rs_done[1]),
                           ("dve", rs_done[2]), ("dve", rs_done[3]),
                           ("pe", ut_mm[0])])
                rs_cast[gset2] = d_cast
                t_mm = None
                for j2 in range(4):
                    t_mm = mm(UT[0:1, j2 * 128:(j2 + 1) * 128], rsb4[:, j2:j2 + 1],
                              identb[:, :], start=True, stop=True,
                              waits=[("dve", d_cast), ut_free_a[0], nbc_free[0]],
                              inc="pe" if j2 == 3 else None)
                d_rr = dve(lambda e: e.reciprocal(rcb[0:1, 0:512],
                           UT[0:1, 0:512]), waits=[("pe", t_mm), rcb_free[0]])
                d_rr2 = dve(lambda e: e.tensor_copy(rrow[0:1, 0:512],
                            rcb[0:1, 0:512]), waits=[("dve", d_rr), rrow_free[0]])
                b_mm = mm(UT, negb[0:1, 0:128], rrow[0:1, 0:512], start=True,
                          stop=True, waits=[("dve", d_rr2)], inc="pe")
                ut_mm[0] = b_mm
                a_rcb = act(lambda e: e.activation(rcb[:, :], UT[:, 0:512],
                            AF.Copy, bias=0.0, scale=-1.0),
                            waits=[("pe", b_mm), rcb_free[0]])
                ut_free_a[0] = ("act", a_rcb)
                rrow_free[0] = ("pe", b_mm)
                d_cx = dve(lambda e, c=bank, q0=qg: e.tensor_tensor(
                    ctxh[:, q0 * TT:(q0 + 1) * TT], c, rcb[:, :], OP.mult),
                    waits=[("pe", pv_last), ("act", a_rcb), ctxh_free[0]])
                rcb_free[0] = ("dve", d_cx)
                ctx_evac[cb] = ("dve", d_cx)
                acc_free[cb] = [("dve", d_cx)]

        for j in range(len(halves)):
            stage1(j)
            if j >= DELAY:
                stage2(j - DELAY)
        for j in range(len(halves) - DELAY, len(halves)):
            stage2(j)
        v = dma("qsp", "sctx", ctx_scr[hg * 128:(hg + 1) * 128, :], ctxh[:, :],
                waits=[ctx_evac[0], ctx_evac[1]])
        ctxh_free[0] = ("sctx", v)
        return v

    ctx_spills = []
    for g in range(NKV):
        P.tag = "grp%d" % g
        dma("qgp", "sbias", bkr, bk_in[0:1, g * 128:(g + 1) * 128],
            waits=[("pe", q_slot_attn_pe[0]), ("pe", q_slot_attn_pe[1])])
        dma("qgp", "sbias", bvr, bv_in[0:1, g * 128:(g + 1) * 128])
        dma("qgp", "sbias", bqr, bq_in[0:1, g * 512:(g + 1) * 512])
        bias_q = P.at("sbias")

        sl = load_slab(wk_in[g * 128:(g + 1) * 128, :])

        def evac_k(bank, pv, tt):
            return dve(lambda e, b=bank, t=tt: e.tensor_copy(
                kK[:, t * TT:(t + 1) * TT], b),
                waits=[("pe", pv), ("pe", q_slot_attn_pe[0]),
                       ("pe", q_slot_attn_pe[1])])

        lp = proj_chunk(sl, bkr[0:1, :], evac_k, extra=[("sbias", bias_q)])
        wb_free[sl] = [("pe", lp)]
        d_kropes = rope_inplace(kK, pre_waits=[("pe", q_slot_attn_pe[0]),
                                               ("pe", q_slot_attn_pe[1])])
        d_krope = d_kropes[1]

        sl = load_slab(wv_in[g * 128:(g + 1) * 128, :])
        vf = psb[0]

        def evac_v(bank, pv, tt):
            return dve(lambda e, b=bank, t=tt: e.tensor_copy(
                vf[:, t * TT:(t + 1) * TT], b),
                waits=[("pe", pv)] + ph_free[0] + ph_free[1])

        lp = proj_chunk(sl, bvr[0:1, :], evac_v, extra=[("sbias", bias_q)])
        wb_free[sl] = [("pe", lp)]
        v_fm = P.at("dve")
        v_tr = []
        t_last = None
        for b0 in range(0, 16, 8):
            for kc in range(b0, b0 + 8):
                t_last = P.op("pe", lambda e, k=kc, j=(kc - b0): e.transpose(
                    tpb[:, j * 128:(j + 1) * 128], vf[:, k * 128:(k + 1) * 128],
                    identb[:, :]),
                    waits=[("dve", v_fm), c_const, tp_free[0], pt_free[0]],
                    inc="pe" if kc == b0 + 7 else None)
            a_v = act(lambda e, b0=b0: e.copy(kV[:, b0 * 128:(b0 + 8) * 128],
                      tpb[:, 0:1024]), waits=[("pe", t_last)])
            tp_free[0] = ("act", a_v)
            v_tr.append(a_v)
        ph_free[0] = [("pe", t_last)]
        ph_free[1] = [("pe", t_last)]

        km = {0: d_kropes[0], 1: d_kropes[0], 2: d_kropes[1], 3: d_kropes[1]}
        pe_k2 = sq_row_mms(kK, km,
                           extra=reg_free[0] + reg_free[1] + [rowb_free[0]])
        d_ke = dve(lambda e: e.tensor_copy(rowb[:, :], ps7[0:1, 0:2048]),
                   waits=[("pe", pe_k2[-1]), rowb_free[0]])
        d_km = dve(lambda e: e.reduce_max(kmaxf[0:1, 0:1], rowb[:, :], axis=AX.X),
                   waits=[("dve", d_ke)])
        kmax_mark[0] = d_km
        rowb_free[0] = ("dve", d_km)
        d_kv = dve(lambda e: e.engine_nop(),
                   waits=[("dve", d_krope), ("act", v_tr[-1])])
        kv_mark[0] = d_kv

        for hh in range(RQ):
            qs = hh % 2
            qb = kQ[qs]
            sl = load_slab(wq_in[(4 * g + hh) * 128:(4 * g + hh + 1) * 128, :])

            qev = {}

            def evac_q2(bank, pv, tt, q=qb, qs=qs):
                d = dve(lambda e, b=bank, t=tt, q=q: e.tensor_scalar(
                    q[:, t * TT:(t + 1) * TT], b, QSCALE, None, OP.mult),
                    waits=[("pe", pv), ("pe", q_slot_attn_pe[qs])])
                qev[tt] = ("dve", d)
                return d

            lp = proj_chunk(sl, bqr[0:1, hh * 128:(hh + 1) * 128], evac_q2,
                            extra=[("sbias", bias_q)])
            wb_free[sl] = [("pe", lp)]
            d_qrope = rope_inplace(qb, pre_waits=[("pe", q_slot_attn_pe[qs])],
                                   tt_ready=qev)
            v = attention_head(g, hh, qb, d_qrope)
            ctx_spills.append(v)
            q_slot_attn_pe[qs] = P.at("pe")

        rowb_free[0] = ("dve", P.at("dve"))

    pe_attn_end = P.at("pe")
    dve_attn_end = P.at("dve")
    act_attn_end = P.at("act")

    P.tag = "p4"
    for dc in range(NDC):
        dma("qsp", "scxr", xh(dc), ctx_scr[dc * 128:(dc + 1) * 128, :],
            waits=[("pe", pe_attn_end), ("dve", dve_attn_end),
                   ("sctx", P.at("sctx"))])
    ctx_rd_all = P.at("scxr")
    bor = av(o_kvq, 1, DM)
    v_bo = dma("qgp", "sbias", bor, bo_in[0:1, :], waits=[("pe", pe_attn_end)])
    apst = [psb[0], psb[1]]
    ap_spill = []
    apst_free = [[("act", act_attn_end), ("pe", pe_attn_end)],
                 [("act", act_attn_end), ("pe", pe_attn_end)]]
    for ob in range(NDC):
        sl = load_slab(wo_in[ob * 128:(ob + 1) * 128, :])
        st = ob % 2
        dvl = None
        for tt in range(NTT):
            bi = acc_turn[0] % 2
            acc_turn[0] += 1
            bank = [CXA, CXB][bi]
            for kc in range(NDC):
                wv_ = [wb_loaded[sl], ("scxr", ctx_rd_all), ("sbias", v_bo)]
                if kc == 0:
                    wv_ = wv_ + acc_free[bi]
                mm(bank, wbuf[sl][:, kc * 128:(kc + 1) * 128],
                   xh(kc)[:, tt * TT:(tt + 1) * TT], start=(kc == 0), stop=False,
                   waits=wv_)
            pv = mm(bank, bor[0:1, ob * 128:(ob + 1) * 128], onesr[0:1, 0:TT],
                    start=False, stop=True, inc="pe")
            dvl = dve(lambda e, b=bank, t=tt, s=st: e.tensor_copy(
                apst[s][:, t * TT:(t + 1) * TT], b),
                waits=[("pe", pv)] + apst_free[st])
            acc_free[bi] = [("dve", dvl)]
            wb_free[sl] = [("pe", pv)]
        v = dma("qsp", "sap%d" % st, ap_scr[ob * 128:(ob + 1) * 128, :],
                apst[st][:, :], waits=[("dve", dvl)])
        ap_spill.append(v)
        apst_free[st] = [("sap%d" % st, v)]

    pe_p4_end = P.at("pe")
    ap_all = [("sap0", P.at("sap0")), ("sap1", P.at("sap1"))]

    P.tag = "p5"
    o5_apb = o_wbuf
    o5_sqs = o_wbuf + 8192
    o5_xsb = o_wbuf + 10240
    o5_tsc = o_wbuf + 26624
    o5_xps = o_wbuf + 34816
    o5_mb = o_wbuf + 43008
    o5_rb = o_wbuf + 47104
    assert o_wbuf + 51200 <= o_rowa
    apb = [av(o5_apb, 128, S), av(o5_apb + 4096, 128, S)]
    sqs5 = [av(o5_sqs, 128, TT), av(o5_sqs + 1024, 128, TT)]
    xsb = [av(o5_xsb, 128, S, F32), av(o5_xsb + 8192, 128, S, F32)]
    tsc5 = [av(o5_tsc, 128, S), av(o5_tsc + 4096, 128, S)]
    xps5 = [av(o5_xps, 128, S), av(o5_xps + 4096, 128, S)]
    mbt2 = av(o5_mb, 128, S)
    rbt2 = av(o5_rb, 128, S)

    def stats_ln2(pre_waits):
        apb2 = [av(o_kvq + 8192, 128, S), av(o_kvq + 12288, 128, S)]
        sqs2 = [av(o_ropeA, 128, TT), av(o_ropeB, 128, TT)]
        ev_last = 0
        buf_free = [list(pre_waits), list(pre_waits)]
        sq_free = [list(pre_waits), list(pre_waits)]
        for half in range(2):
            for dc in range(NDC):
                i = dc % 2
                v = dma("qsp", "sab%d" % i, apb2[i],
                        ap_scr[dc * 128:(dc + 1) * 128, :],
                        waits=buf_free[i] + [("sap%d" % (dc % 2), ap_spill[dc])])
                for tt in (half * 2, half * 2 + 1):
                    sum_ps = ps7[0:1, (tt % 2) * TT:(tt % 2 + 1) * TT]
                    sq_ps = ps7[0:1, 1024 + (tt % 2) * TT:1024 + (tt % 2 + 1) * TT]
                    wv_ = [("sab%d" % i, v)]
                    if dc == 0:
                        wv_ += list(pre_waits) + [("dve", ev_last)]
                    mm(sum_ps, onescolb[:, :], apb2[i][:, tt * TT:(tt + 1) * TT],
                       start=(dc == 0), stop=(dc == NDC - 1), waits=wv_,
                       inc="pe" if dc == NDC - 1 else None)
                    si = dc % 2
                    dd = P.op("pool", lambda e, s=sqs2[si], b=apb2[i], t=tt:
                              e.tensor_tensor(s, b[:, t * TT:(t + 1) * TT],
                              b[:, t * TT:(t + 1) * TT], OP.mult),
                              waits=[("sab%d" % i, v)] + sq_free[si], inc="pool")
                    pv2 = mm(sq_ps, onescolb[:, :], sqs2[si], start=(dc == 0),
                             stop=(dc == NDC - 1),
                             waits=[("pool", dd)] + (wv_ if dc == 0 else []),
                             inc="pe")
                    sq_free[si] = [("pe", pv2)]
                buf_free[i] = [("pe", P.at("pe"))]
            pe_h = P.at("pe")
            for tt in (half * 2, half * 2 + 1):
                dve(lambda e, t=tt: e.tensor_copy(rowa[0:1, t * TT:(t + 1) * TT],
                    ps7[0:1, (t % 2) * TT:(t % 2 + 1) * TT]), waits=[("pe", pe_h)])
                ev_last = dve(lambda e, t=tt: e.tensor_copy(
                    rowb[0:1, t * TT:(t + 1) * TT],
                    ps7[0:1, 1024 + (t % 2) * TT:1024 + (t % 2 + 1) * TT]),
                    waits=[("pe", pe_h)])
        return ev_last

    pre_w = [("pe", pe_attn_end), ("act", act_attn_end), ("dve", dve_attn_end),
             rowa_free[0], rowb_free[0]]
    s_ev = stats_ln2(pre_w)
    if phase_limit == "rawstats":
        dma("qgp", "sdbg", out[256:257, :], rowb[0:1, :], waits=[("dve", s_ev)])
        dma("qgp", "sdbg", out[257:258, :], rowa[0:1, :], waits=[("dve", s_ev)])
        return _finish(nc, es, P, sems)
    a_r2 = row_norm_stats(s_ev)
    a_bc2 = bcast_rows(mbt2, rbt2, a_r2,
                       extra=[("pe", pe_p4_end), ("sctx", P.at("sctx"))] + ap_all)
    rowa_free[0] = ("act", a_bc2)
    rowb_free[0] = ("act", a_bc2)

    if phase_limit == "ln2":
        dma("qgp", "sdbg", out[0:128, :], rbt2, waits=[("act", a_bc2)])
        dma("qgp", "sdbg", out[128:256, :], mbt2, waits=[("act", a_bc2)])
        dma("qgp", "sdbg", out[256:257, :], rowb[0:1, :], waits=[("act", a_bc2)])
        dma("qgp", "sdbg", out[257:258, :], rowa[0:1, :], waits=[("act", a_bc2)])
        return _finish(nc, es, P, sems)

    xp_spill = []
    b_free = [[("act", a_bc2), ("pe", P.at("pe"))],
              [("act", a_bc2), ("pe", P.at("pe"))]]
    x_free = [[("pe", pe_p4_end)], [("pe", pe_p4_end)]]
    t_free = [[("pe", pe_p4_end), ("dve", P.at("dve"))],
              [("pe", pe_p4_end), ("dve", P.at("dve"))]]
    xp_free = [ap_all + [("pe", pe_p4_end)],
               ap_all + [("pe", pe_p4_end)]]
    sq3_banks = [UT[0:1, 0:512], CXA[0:1, 0:512], CXB[0:1, 0:512],
                 tpb.ap()[0:1, 0:1024].bitcast(F32)[0:1, 0:512]]
    sqp_free = [[("pe", P.at("pe")), ("act", act_attn_end)],
                [("pe", P.at("pe")), ("act", act_attn_end)]]
    st3_pre = [("act", a_bc2), ("dve", P.at("dve")), ("pe", pe_p4_end)]
    last_sum3 = {}
    for dc in range(NDC):
        i = dc % 2
        v1 = dma("qsp", "sab%d" % i, apb[i], ap_scr[dc * 128:(dc + 1) * 128, :],
                 waits=b_free[i] + ap_all)
        v2 = dma("qsp", "sxf%d" % i, xsb[i], x_in[dc * 128:(dc + 1) * 128, :],
                 waits=x_free[i])
        d1 = dve(lambda e, i=i: e.tensor_tensor(tsc5[i], apb[i], mbt2[:, :],
                 OP.subtract), waits=[("sab%d" % i, v1), ("act", a_bc2)] + t_free[i])
        d2 = dve(lambda e, i=i: e.tensor_tensor(xps5[i], tsc5[i], rbt2[:, :],
                 OP.mult), waits=[("dve", d1)] + xp_free[i])
        d3 = dve(lambda e, i=i: e.tensor_tensor(xps5[i], xps5[i], xsb[i], OP.add),
                 waits=[("dve", d2), ("sxf%d" % i, v2)])
        v3 = dma("qsp", "sxp%d" % i, xp_scr[dc * 128:(dc + 1) * 128, :], xps5[i],
                 waits=[("dve", d3)])
        xp_spill.append(v3)
        for tt in range(NTT):
            wv_ = [("dve", d3)] + (st3_pre if dc == 0 else [])
            last_sum3[tt] = mm(ps7[0:1, tt * TT:(tt + 1) * TT], onescolb[:, :],
                               xps5[i][:, tt * TT:(tt + 1) * TT],
                               start=(dc == 0), stop=(dc == NDC - 1), waits=wv_,
                               inc="pe" if dc == NDC - 1 else None)
            sp = tt % 2
            pp = P.op("pool", lambda e, s=sqs5[sp], i=i, t=tt: e.tensor_tensor(
                s, xps5[i][:, t * TT:(t + 1) * TT],
                xps5[i][:, t * TT:(t + 1) * TT], OP.mult),
                waits=[("dve", d3)] + sqp_free[sp], inc="pool")
            pq = mm(sq3_banks[tt], onescolb[:, :], sqs5[sp],
                    start=(dc == 0), stop=(dc == NDC - 1),
                    waits=[("pool", pp)] + (st3_pre if dc == 0 else []),
                    inc="pe")
            sqp_free[sp] = [("pe", pq)]
        b_free[i] = [("dve", d1)]
        x_free[i] = [("dve", d3)]
        t_free[i] = [("dve", d2)]
        xp_free[i] = [("sxp%d" % i, v3), ("pe", P.at("pe"))]

    xp_all = [("sxp0", P.at("sxp0")), ("sxp1", P.at("sxp1"))]
    pe_st3 = P.at("pe")
    s_ev3 = 0
    for tt in range(NTT):
        dve(lambda e, t=tt: e.tensor_copy(rowa[0:1, t * TT:(t + 1) * TT],
            ps7[0:1, t * TT:(t + 1) * TT]), waits=[("pe", pe_st3), rowa_free[0]])
        s_ev3 = dve(lambda e, t=tt: e.tensor_copy(rowb[0:1, t * TT:(t + 1) * TT],
                    sq3_banks[t]), waits=[("pe", pe_st3), rowb_free[0]])
    a_r3 = row_norm_stats(s_ev3)
    a_bc3 = bcast_rows(mbt2, rbt2, a_r3)
    rowa_free[0] = ("act", a_bc3)
    rowb_free[0] = ("act", a_bc3)

    h2_marks = []
    hb_free = [[("act", a_bc3), ("pe", P.at("pe"))],
               [("act", a_bc3), ("pe", P.at("pe"))]]
    t2_free = [[("dve", P.at("dve"))], [("dve", P.at("dve"))]]
    for dc in range(NDC):
        i = dc % 2
        v1 = dma("qsp", "sab%d" % i, apb[i], xp_scr[dc * 128:(dc + 1) * 128, :],
                 waits=hb_free[i] + xp_all)
        wv_ = [("sab%d" % i, v1), ("act", a_bc3), ("pe", pe_p4_end)] + t2_free[i]
        if dc >= 2:
            wv_.append(("dve", h2_marks[dc - 2]))
        d1 = dve(lambda e, i=i: e.tensor_tensor(tsc5[i], apb[i], mbt2[:, :],
                 OP.subtract), waits=wv_)
        d2 = dve(lambda e, i=i, d=dc: e.tensor_tensor(xh(d), tsc5[i], rbt2[:, :],
                 OP.mult), waits=[("dve", d1)])
        h2_marks.append(d2)
        hb_free[i] = [("dve", d1)]
        t2_free[i] = [("dve", d2)]
    h2_all = h2_marks[-1]

    if phase_limit == "h2":
        for dc in range(NDC):
            dma("qgp", "sdbg", out[dc * 128:(dc + 1) * 128, :], xh(dc),
                waits=[("dve", h2_all)])
        return _finish(nc, es, P, sems)

    P.tag = "p6"
    o6_w = o_wbuf
    o6_wdc = o_wbuf + 32768
    o6_g = o6_wdc + 22016
    o6_u = o6_g + 2048
    o6_m = o6_u + 2048
    assert o6_m + 8192 <= AB
    wb6 = [av(o6_w + i * 8192, 128, 4096) for i in range(4)]
    wdc = av(o6_wdc, 128, FF)
    gst = [av(o6_g, 128, TT), av(o6_g + 1024, 128, TT)]
    ust = [av(o6_u, 128, TT), av(o6_u + 1024, 128, TT)]
    mst = [av(o6_m, 128, S), av(o6_m + 4096, 128, S)]

    wb6_free = [[("pe", P.at("pe")), ("dve", P.at("dve"))] for _ in range(4)]
    m_spills = []
    mst_free = [[("dve", P.at("dve")), ("pe", P.at("pe"))],
                [("dve", P.at("dve")), ("pe", P.at("pe"))]]
    g_free = [[("dve", P.at("dve")), ("pe", P.at("pe"))],
              [("dve", P.at("dve")), ("pe", P.at("pe"))]]
    u_free = [[("dve", P.at("dve")), ("pe", P.at("pe"))],
              [("dve", P.at("dve")), ("pe", P.at("pe"))]]
    acc_free6 = [[("dve", s_ev3)], [("dve", s_ev3)]]
    wdc_q = [0]
    wd_conv = []
    m_dst = m_scr.rearrange("t p j -> p t j")

    for fb in range(NFC):
        P.tag = "p6_f%d" % fb
        s0 = (fb % 2) * 2
        sg = ("sg0", "sg1")[fb % 2]
        su = ("su0", "su1")[fb % 2]
        v_g = dma("qgp", sg, wb6[s0], wg_in[fb * 128:(fb + 1) * 128, :],
                  waits=wb6_free[s0] + ([("dve", h2_all)] if fb == 0 else []))
        v_u = dma("qgp", su, wb6[s0 + 1], wu_in[fb * 128:(fb + 1) * 128, :],
                  waits=wb6_free[s0 + 1])
        while len(wd_conv) * NFC <= fb * NDC and len(wd_conv) < NDC:
            db = len(wd_conv)
            vv = dma("qgp", "scv", wdc, wd_in[db * 128:(db + 1) * 128, :],
                     waits=[("scw", wdc_q[0])] + ([("pe", pe_p4_end),
                     ("dve", P.at("dve"))] if db == 0 else []))
            vw = dma("qsp", "scw", wd_scr[db * 128:(db + 1) * 128, :], wdc,
                     waits=[("scv", vv)])
            wdc_q[0] = vw
            wd_conv.append(vw)
        gp = None
        up = None
        for tt in range(NTT):
            i = tt % 2
            for kc in range(NDC):
                wv_ = [(sg, v_g), ("dve", h2_marks[kc])]
                if kc == 0:
                    wv_ = wv_ + acc_free6[0]
                gp = mm(CXA, wb6[s0][:, kc * 128:(kc + 1) * 128],
                        xh(kc)[:, tt * TT:(tt + 1) * TT], start=(kc == 0),
                        stop=(kc == NDC - 1), waits=wv_,
                        inc="pe" if kc == NDC - 1 else None)
            a_g = act(lambda e, i=i: e.activation(gst[i], CXA, AF.Silu, bias=0.0,
                      scale=1.0), waits=[("pe", gp)] + g_free[i])
            acc_free6[0] = [("act", a_g)]
            for kc in range(NDC):
                wv_ = [(su, v_u), ("dve", h2_marks[kc])]
                if kc == 0:
                    wv_ = wv_ + acc_free6[1]
                up = mm(CXB, wb6[s0 + 1][:, kc * 128:(kc + 1) * 128],
                        xh(kc)[:, tt * TT:(tt + 1) * TT], start=(kc == 0),
                        stop=(kc == NDC - 1), waits=wv_,
                        inc="pe" if kc == NDC - 1 else None)
            d_u = dve(lambda e, i=i: e.tensor_copy(ust[i], CXB),
                      waits=[("pe", up)] + u_free[i])
            acc_free6[1] = [("dve", d_u)]
            d_m = dve(lambda e, i=i, t=tt, f=(fb % 2): e.tensor_tensor(
                mst[f][:, t * TT:(t + 1) * TT], gst[i], ust[i], OP.mult),
                waits=[("act", a_g), ("dve", d_u)] + mst_free[fb % 2])
            g_free[i] = [("dve", d_m)]
            u_free[i] = [("dve", d_m)]
        wb6_free[s0] = [("pe", gp)]
        wb6_free[s0 + 1] = [("pe", up)]
        v_m = dma("qsp", ("sm0", "sm1")[fb % 2],
                  m_dst[:, :, fb * TT:(fb + 1) * TT], mst[fb % 2],
                  waits=[("dve", P.at("dve"))])
        m_spills.append(v_m)
        mst_free[fb % 2] = [(("sm0", "sm1")[fb % 2], v_m), ("dve", P.at("dve"))]
    pe_p6_end = P.at("pe")
    m_all = [("sm0", P.at("sm0")), ("sm1", P.at("sm1"))]
    wd_all = [("scw", P.at("scw"))]

    P.tag = "p7"
    o7_mtt = 0
    o7_wd = 90112
    o7_ytt = 134144
    o7_xps = 166912
    o7_outs = 169984
    o7_sq = 174080
    o7_mb = 177152
    assert o7_mb + 6144 <= o_rowa
    mtt = av(o7_mtt, 128, NFC * TT)
    wd7 = [av(o7_wd, 128, FF), av(o7_wd + 22016, 128, FF)]
    ytt = av(o7_ytt, 128, NDC * TT)
    xps7 = [av(o7_xps, 128, TT), av(o7_xps + 1024, 128, TT)]
    outs7 = [av(o7_outs, 128, TT, F32), av(o7_outs + 2048, 128, TT, F32)]
    sq7 = [av(o7_sq, 128, TT), av(o7_sq + 1024, 128, TT)]
    esq7 = [av(o7_mb + 4096, 128, TT), av(o7_mb + 5120, 128, TT)]
    m4bs = [av(o7_mb, 128, TT, F32), av(o_rowa + 1024, 128, TT, F32)]
    r4bs = [av(o7_mb + 2048, 128, TT, F32), av(o_rowb + 1024, 128, TT, F32)]

    dve_p7s = P.at("dve")
    wd7_free = [[("pe", pe_p6_end)], [("pe", pe_p6_end)]]
    mtt_q = [0]
    ytt_free = [[("pe", pe_p6_end)] for _ in range(NDC)]
    xps7_free = [[("scw", P.at("scw")), ("dve", dve_p7s)],
                 [("scw", P.at("scw")), ("dve", dve_p7s)]]
    outs7_free = [[("scw", P.at("scw")), ("dve", dve_p7s)],
                  [("scw", P.at("scw")), ("dve", dve_p7s)]]
    ut_free7 = [P.at("act")]
    rows_free7 = [P.at("dve")]
    sq7_init = [("scw", P.at("scw")), ("dve", dve_p7s)]

    mb_free = {0: [], 1: []}
    epi_state = [None]
    epi_marks = {}
    epi_ix = [0]

    def plan_epilogue_chunk():
        if epi_state[0] is None:
            return
        (ett, y_marks_e, a_m4e, a_r4e, m4be, r4be) = epi_state[0]
        dc = epi_ix[0]
        if dc >= NDC:
            return
        epi_ix[0] += 1
        i = dc % 2
        v_x = dma("qsp", ("sx70", "sx71")[i], xps7[i],
                  xp_scr[dc * 128:(dc + 1) * 128, ett * TT:(ett + 1) * TT],
                  waits=xps7_free[i] + xp_all)
        dd1 = P.op("pool", lambda e, i=i, d=dc, m=m4be: e.tensor_tensor(
            esq7[i], ytt[:, d * TT:(d + 1) * TT], m, OP.subtract),
            waits=[("act", a_m4e), ("dve", y_marks_e[dc])] + esq_free[i],
            inc="pool")
        dd2 = P.op("pool", lambda e, i=i, r=r4be: e.tensor_tensor(
            esq7[i], esq7[i], r, OP.mult),
            waits=[("pool", dd1), ("act", a_r4e)], inc="pool")
        dd3 = P.op("pool", lambda e, i=i: e.tensor_tensor(
            outs7[i], esq7[i], xps7[i], OP.add),
            waits=[("pool", dd2), (("sx70", "sx71")[i], v_x)] + outs7_free[i],
            inc="pool")
        v_o = dma("qsp", ("so0", "so1")[i],
                  out[dc * 128:(dc + 1) * 128, ett * TT:(ett + 1) * TT],
                  outs7[i], waits=[("pool", dd3)])
        xps7_free[i] = [("pool", dd3)]
        esq_free[i] = [("pool", dd3)]
        outs7_free[i] = [(("so0", "so1")[i], v_o)]
        epi_marks[(ett, dc)] = dd1

    esq_free = [[("scw", P.at("scw"))], [("scw", P.at("scw"))]]

    for tt in range(NTT):
        P.tag = "p7_t%d" % tt
        v_m = dma("qgp", "smt", mtt[:, :], m_scr[tt, :, :],
                  waits=m_all + [("pe", pe_p6_end if tt == 0 else mtt_q[0])])
        sum_ps = ps7[0:1, 2048:2560]
        sq_ps = ps7[0:1, 0:512]
        y_marks = {}
        sq_free7 = [list(sq7_init), list(sq7_init)]
        stats_last = None
        pv = None
        m4b = m4bs[tt % 2]
        r4b = r4bs[tt % 2]
        for db in range(NDC):
            plan_epilogue_chunk()
            sl = db % 2
            v_w = dma("qgp", ("swd0", "swd1")[sl],
                      wd7[sl], wd_scr[db * 128:(db + 1) * 128, :],
                      waits=wd7_free[sl] + wd_all)
            bi = db % 2
            bank = [CXA, CXB][bi]
            for fc in range(NFC):
                wv_ = [(("swd0", "swd1")[sl], v_w), ("smt", v_m)]
                if fc == 0:
                    wv_ = wv_ + acc_free6[bi]
                pv = mm(bank, wd7[sl][:, fc * 128:(fc + 1) * 128],
                        mtt[:, fc * TT:(fc + 1) * TT], start=(fc == 0),
                        stop=(fc == NFC - 1), waits=wv_,
                        inc="pe" if fc == NFC - 1 else None)
            wd7_free[sl] = [("pe", pv)]
            yw = [("pe", pv)] + ytt_free[db]
            if tt > 0:
                yw.append(("pool", epi_marks[(tt - 1, db)]))
            d_y = dve(lambda e, b=bank, d=db: e.tensor_copy(
                ytt[:, d * TT:(d + 1) * TT], b), waits=yw)
            acc_free6[bi] = [("dve", d_y)]
            y_marks[db] = d_y
            mm(sum_ps, onescolb[:, :], ytt[:, db * TT:(db + 1) * TT],
               start=(db == 0), stop=(db == NDC - 1),
               waits=[("dve", d_y)] + ([("act", ut_free7[0]),
               ("dve", rows_free7[0])] if db == 0 else []),
               inc="pe" if db == NDC - 1 else None)
            dd = dve(lambda e, s=sq7[db % 2], d=db: e.tensor_tensor(
                s, ytt[:, d * TT:(d + 1) * TT], ytt[:, d * TT:(d + 1) * TT],
                OP.mult), waits=[("dve", d_y)] + sq_free7[db % 2])
            pv2 = mm(sq_ps, onescolb[:, :], sq7[db % 2], start=(db == 0),
                     stop=(db == NDC - 1),
                     waits=[("dve", dd)] + (reg_free[0] if db == 0 and tt == 0
                                            else []),
                     inc="pe")
            sq_free7[db % 2] = [("pe", pv2)]
            stats_last = P.at("pe")
        mtt_q[0] = pv
        d1 = dve(lambda e: e.tensor_copy(rowa[0:1, 0:TT], ps7[0:1, 2048:2560]),
                 waits=[("pe", stats_last), rowa_free[0]])
        d2 = dve(lambda e: e.tensor_copy(rowb[0:1, 0:TT], ps7[0:1, 0:512]),
                 waits=[("pe", stats_last), rowb_free[0]])
        rowa_free[0] = ("dve", d1)
        a_r4r = row_norm_stats(max(d1, d2), n=TT,
                               m2_waits=[("dve", rows_free7[0])])
        pvb = mm(UT, negb[0:1, 0:128], rowa[0:1, 0:TT], start=True, stop=True,
                 waits=[("act", a_r4r), ("act", ut_free7[0]), ("pool", c_mem)],
                 inc="pe")
        a_m4 = act(lambda e, m=m4b: e.activation(m[:, :], UT[:, 0:TT], AF.Copy,
                   bias=0.0, scale=-1.0), waits=[("pe", pvb)] + mb_free[tt % 2])
        pvb2 = mm(UT, negb[0:1, 0:128], rowb[0:1, 0:TT], start=True, stop=True,
                  waits=[("act", a_m4)], inc="pe")
        a_r4 = act(lambda e, r=r4b: e.activation(r[:, :], UT[:, 0:TT], AF.Copy,
                   bias=0.0, scale=-1.0), waits=[("pe", pvb2)] + mb_free[tt % 2])
        ut_free7[0] = a_r4
        rows_free7[0] = P.at("dve")
        rowa_free[0] = ("pe", pvb)
        rowb_free[0] = ("pe", pvb2)
        while epi_state[0] is not None and epi_ix[0] < NDC:
            plan_epilogue_chunk()
        epi_state[0] = (tt, dict(y_marks), a_m4, a_r4, m4b, r4b)
        epi_ix[0] = 0
        mb_free[tt % 2] = []
    while epi_ix[0] < NDC:
        plan_epilogue_chunk()

    return _finish(nc, es, P, sems)


def _finish(nc, es, P, sems):
    with nc.Block() as block:
        @block.tensor
        def _(te):
            P.emit("pe", te, sems)

        @block.scalar
        def _(sc):
            P.emit("act", sc, sems)

        @block.vector
        def _(v):
            P.emit("dve", v, sems)

        @block.gpsimd
        def _(g):
            P.emit("pool", g, sems)

        @block.sync
        def _(sy):
            P.emit("sp", sy, sems)
            for s in P.cnt:
                if s not in ("pe", "act", "dve", "pool") and P.at(s) > 0:
                    sy.wait_ge(sems[s], P.at(s))
    es.close()
    return nc


def kernel(**inputs):
    in_maps = _host_layout(inputs)
    nc = build_nc()
    res = run_bass_kernel_spmd(nc, in_maps, core_ids=list(range(8)))
    outs = []
    for c in range(8):
        outs.append(np.ascontiguousarray(res.results[c]["out"].T))
    return np.stack(outs, 0)


# version 07
# speedup vs baseline: 1.2725x; 1.0450x over previous
import contextlib
import numpy as np
import ml_dtypes
import concourse.bass as bass
import concourse.mybir as mybir
from concourse.bass_utils import run_bass_kernel_spmd

F32 = mybir.dt.float32
BF16 = mybir.dt.bfloat16
AF = mybir.ActivationFunctionType
AX = mybir.AxisListType
OP = mybir.AluOpType

S = 2048
DM = 4096
NKV = 8
RQ = 4
FF = 11008
NDC = DM // 128
NFC = FF // 128
NTT = 4
TT = 512
NQT = S // 128
EPS = 1e-5
QSCALE = float(1.0 / np.sqrt(128.0))
EXPC = 10.0


class Prog:
    def __init__(self, sems):
        self.recs = []
        self.cnt = {k: 0 for k in sems}
        self.tag = ""

    def op(self, eng, fn, waits=(), inc=None, inc_n=1):
        ws = tuple((s, v) for (s, v) in waits if v and v > 0)
        self.recs.append((eng, ws, fn, inc, inc_n, self.tag))
        if inc is not None:
            self.cnt[inc] += inc_n
            return self.cnt[inc]
        return None

    def at(self, sem):
        return self.cnt[sem]

    def emit(self, eng_name, eng, sems):
        waited = {}
        for (e, ws, fn, inc, inc_n, tag) in self.recs:
            if e != eng_name:
                continue
            for (s, v) in ws:
                if waited.get(s, 0) < v:
                    eng.wait_ge(sems[s], v)
                    waited[s] = v
            inst = fn(eng)
            if tag:
                inst.annotate(tag)
            if inc is not None:
                inst.then_inc(sems[inc], inc_n)


def _host_layout(inputs):
    x = np.asarray(inputs["L0_attn_hidden_states"], dtype=np.float32)

    def proj_slabs(w, nob):
        a = np.asarray(w, dtype=np.float32).reshape(NDC, 128, nob, 128)
        return np.ascontiguousarray(a.transpose(2, 1, 0, 3)).reshape(nob * 128, NDC * 128)

    def ffn_in_slabs(w):
        a = np.asarray(w, dtype=np.float32).reshape(NFC, 128, NDC, 128).transpose(0, 3, 2, 1)
        return np.ascontiguousarray(a).reshape(NFC * 128, NDC * 128)

    def ffn_out_slabs(w):
        a = np.asarray(w, dtype=np.float32).reshape(NDC, 128, NFC, 128).transpose(0, 3, 2, 1)
        return np.ascontiguousarray(a).reshape(NDC * 128, NFC * 128)

    shared = {
        "wq": proj_slabs(inputs["L0_attn_Wq"], 32),
        "wk": proj_slabs(inputs["L0_attn_Wk"], 8),
        "wv": proj_slabs(inputs["L0_attn_Wv"], 8),
        "wo": proj_slabs(inputs["L0_attn_Wo"], 32),
        "wg": ffn_in_slabs(inputs["L0_ffn_W_gate"]),
        "wu": ffn_in_slabs(inputs["L0_ffn_W_up"]),
        "wd": ffn_out_slabs(inputs["L0_ffn_W_down"]),
        "bq": np.ascontiguousarray(inputs["L0_attn_bq"], dtype=np.float32).reshape(1, DM),
        "bk": np.ascontiguousarray(inputs["L0_attn_bk"], dtype=np.float32).reshape(1, 1024),
        "bv": np.ascontiguousarray(inputs["L0_attn_bv"], dtype=np.float32).reshape(1, 1024),
        "bo": np.ascontiguousarray(inputs["L0_attn_bo"], dtype=np.float32).reshape(1, DM),
    }
    in_maps = []
    for c in range(8):
        m = dict(shared)
        m["x"] = np.ascontiguousarray(x[c].T)
        in_maps.append(m)
    return in_maps


def _consts():
    half = 64
    inv = 1.0 / (10000.0 ** (np.arange(half, dtype=np.float64) / half))
    ang = np.arange(S, dtype=np.float64)[None, :] * inv[:, None]
    cos = np.cos(ang).astype(np.float32)
    sin = np.sin(ang).astype(np.float32)
    cos_d = np.concatenate([cos, cos], 0).astype(ml_dtypes.bfloat16)
    sin_d = np.concatenate([sin, sin], 0).astype(ml_dtypes.bfloat16)
    ident = np.eye(128, dtype=ml_dtypes.bfloat16)
    q_ = np.arange(128)[:, None]
    k_ = np.arange(128)[None, :]
    tri = np.where(k_ <= q_, 0.0, -1e30).astype(np.float32)
    return cos_d, sin_d, ident, tri


def build_nc(phase_limit=None):
    nc = bass.Bass("TRN2", target_bir_lowering=False)
    es = contextlib.ExitStack()

    x_in = nc.dram_tensor("x", [DM, S], F32, kind="ExternalInput").ap()
    wq_in = nc.dram_tensor("wq", [32 * 128, NDC * 128], F32, kind="ExternalInput").ap()
    wk_in = nc.dram_tensor("wk", [8 * 128, NDC * 128], F32, kind="ExternalInput").ap()
    wv_in = nc.dram_tensor("wv", [8 * 128, NDC * 128], F32, kind="ExternalInput").ap()
    wo_in = nc.dram_tensor("wo", [32 * 128, NDC * 128], F32, kind="ExternalInput").ap()
    wg_in = nc.dram_tensor("wg", [NFC * 128, NDC * 128], F32, kind="ExternalInput").ap()
    wu_in = nc.dram_tensor("wu", [NFC * 128, NDC * 128], F32, kind="ExternalInput").ap()
    wd_in = nc.dram_tensor("wd", [NDC * 128, NFC * 128], F32, kind="ExternalInput").ap()
    bq_in = nc.dram_tensor("bq", [1, DM], F32, kind="ExternalInput").ap()
    bk_in = nc.dram_tensor("bk", [1, 1024], F32, kind="ExternalInput").ap()
    bv_in = nc.dram_tensor("bv", [1, 1024], F32, kind="ExternalInput").ap()
    bo_in = nc.dram_tensor("bo", [1, DM], F32, kind="ExternalInput").ap()
    out = nc.dram_tensor("out", [DM, S], F32, kind="ExternalOutput").ap()

    ctx_scr = nc.dram_tensor("ctx_scr", [NDC * 128, S], BF16, kind="Internal").ap()
    ap_scr = nc.dram_tensor("ap_scr", [NDC * 128, S], BF16, kind="Internal").ap()
    xp_scr = nc.dram_tensor("xp_scr", [NDC * 128, S], BF16, kind="Internal").ap()
    m_scr = nc.dram_tensor("m_scr", [NTT, 128, NFC * TT], BF16, kind="Internal").ap()
    wd_scr = nc.dram_tensor("wd_scr", [NDC * 128, NFC * 128], BF16, kind="Internal").ap()

    cos_np, sin_np, ident_np, tri_np = _consts()
    cos_t = nc.inline_tensor(cos_np, name="cos_t")
    sin_t = nc.inline_tensor(sin_np, name="sin_t")
    ident_t = nc.inline_tensor(ident_np, name="ident_t")
    tri_t = nc.inline_tensor(tri_np, name="tri_t")
    trib16_t = nc.inline_tensor(tri_np.astype(ml_dtypes.bfloat16), name="trib16_t")
    tribT_t = nc.inline_tensor(np.ascontiguousarray(tri_np.T).astype(ml_dtypes.bfloat16),
                               name="tribT_t")

    identb = es.enter_context(nc.sbuf_tensor("identb", [128, 128], BF16))
    cosb = es.enter_context(nc.sbuf_tensor("cosb", [128, S], BF16))
    sinb = es.enter_context(nc.sbuf_tensor("sinb", [128, S], BF16))
    trib = es.enter_context(nc.sbuf_tensor("trib", [128, 128], F32))
    trib16 = es.enter_context(nc.sbuf_tensor("trib16", [128, 128], BF16))
    tribT = es.enter_context(nc.sbuf_tensor("tribT", [128, 128], BF16))
    onesr = es.enter_context(nc.sbuf_tensor("onesr", [1, TT], BF16))
    negb = es.enter_context(nc.sbuf_tensor("negb", [1, 128], BF16))
    onescolb = es.enter_context(nc.sbuf_tensor("onescolb", [128, 1], BF16))

    ar_bytes = nc.sbuf_bytes_remaining - 128
    ar_elems = (ar_bytes // 512) * 256
    arena = es.enter_context(nc.sbuf_tensor("arena", [128, ar_elems], BF16))
    AB = ar_elems * 2

    def av(off, part, elems, dtype=BF16):
        assert off % 4 == 0
        if dtype == BF16:
            return arena[0:part, off // 2: off // 2 + elems]
        assert dtype == F32
        return arena[0:part, off // 2: off // 2 + 2 * elems].bitcast(F32)

    ps7 = es.enter_context(nc.psum_tensor("ps7", [128, 3584], F32))
    tpb = es.enter_context(nc.psum_tensor("tpb", [128, 1024], BF16))
    R0 = ps7[:, 0:1024]
    R1 = ps7[:, 1024:2048]
    UT = ps7[:, 2048:2560]
    CXA = ps7[:, 2560:3072]
    CXB = ps7[:, 3072:3584]

    dma_sems = ["scst", "sx", "swb0", "swb1", "sbias", "sctx", "scxr",
                "sap0", "sap1", "sab0", "sab1", "sxf0", "sxf1", "sxp0", "sxp1",
                "sg0", "sg1", "su0", "su1", "scv", "scw", "sm0", "sm1",
                "smt", "swd0", "swd1", "sx70", "sx71", "so0", "so1", "sdbg"]
    sem_names = ["pe", "act", "dve", "pool"] + dma_sems
    sems = {s: es.enter_context(nc.semaphore(s)) for s in sem_names}
    P = Prog(sem_names)

    def dma(queue, sem, dst, src, waits=()):
        eng = "pool" if queue == "qgp" else "sp"
        return P.op(eng, lambda e, d=dst, s=src: e.dma_start(d, s), waits=waits,
                    inc=sem, inc_n=16)

    def mm(dst, lhsT, rhs, start, stop, waits=(), inc=None):
        return P.op("pe", lambda e, d=dst, l=lhsT, r=rhs, st=start, sp=stop:
                    e.matmul(d, l, r, start=st, stop=sp), waits=waits, inc=inc)

    def dve(fn, waits=()):
        return P.op("dve", fn, waits=waits, inc="dve")

    def act(fn, waits=()):
        return P.op("act", fn, waits=waits, inc="act")

    dma("qsp", "scst", identb[:, :], ident_t.ap())
    dma("qsp", "scst", cosb[:, :], cos_t.ap())
    dma("qsp", "scst", sinb[:, :], sin_t.ap())
    dma("qsp", "scst", trib[:, :], tri_t.ap())
    dma("qsp", "scst", trib16[:, :], trib16_t.ap())
    dma("qsp", "scst", tribT[:, :], tribT_t.ap())
    c_const = ("scst", P.at("scst"))
    P.op("pool", lambda e: e.memset(onesr[:, :], 1.0), inc="pool")
    P.op("pool", lambda e: e.memset(negb[:, :], -1.0), inc="pool")
    P.op("pool", lambda e: e.memset(onescolb[:, :], 1.0), inc="pool")
    c_mem = P.at("pool")

    o_xh = 0
    o_wbuf = 131072
    o_kvq = 147456
    o_psb = 163840
    o_ptsb = 172032
    o_ctxh = 176128
    o_ropeA = 180224
    o_ropeB = 182272
    o_rowa = 184320
    o_rowb = 188416
    o_nbc = 192512
    o_rs = o_nbc + 64
    o_rc = o_nbc + 96
    o_kmaxf = o_nbc + 112
    o_rs5 = o_nbc + 116
    o_bkr = o_nbc + 160
    o_bvr = o_nbc + 416
    o_bqr = o_nbc + 672
    o_m2 = o_nbc + 1696
    o_rcb = o_m2 + 4160
    assert o_rcb + 2048 <= AB, f"arena {AB} < {o_rcb + 2048}"

    def xh(dc):
        return arena[:, (o_xh // 2) + dc * S:(o_xh // 2) + (dc + 1) * S]

    rowa = av(o_rowa, 1, S)
    rowb = av(o_rowb, 1, S)
    m2row = av(o_m2, 1, S)
    kmaxf = av(o_kmaxf, 1, 1, F32)

    P.tag = "p1"
    mbt = av(o_wbuf, 128, S)
    rbt = av(o_wbuf + 4096, 128, S)
    sqsc = [av(o_kvq + i * 1024, 128, TT) for i in range(2)]

    for dc in range(NDC):
        dma("qgp", "sx", xh(dc), x_in[dc * 128:(dc + 1) * 128, :])
    x_all = P.at("sx")

    st_ev = 0
    sq_slot_pe = [0, 0]
    for tt in range(NTT):
        sum_ps = ps7[0:1, 0:512]
        sq_ps = ps7[0:1, 512:1024]
        for dc in range(NDC):
            wv_ = [("sx", x_all), ("pool", c_mem), ("dve", st_ev)]
            mm(sum_ps, onescolb[:, :], xh(dc)[:, tt * TT:(tt + 1) * TT],
               start=(dc == 0), stop=(dc == NDC - 1), waits=wv_,
               inc="pe" if dc == NDC - 1 else None)
            sl = sqsc[dc % 2]
            dv = dve(lambda e, s=sl, d=dc, t=tt: e.tensor_tensor(
                s, xh(d)[:, t * TT:(t + 1) * TT], xh(d)[:, t * TT:(t + 1) * TT],
                OP.mult), waits=[("sx", x_all), ("pe", sq_slot_pe[dc % 2])])
            pv = mm(sq_ps, onescolb[:, :], sl, start=(dc == 0), stop=(dc == NDC - 1),
                    waits=[("dve", dv), ("dve", st_ev)], inc="pe")
            sq_slot_pe[dc % 2] = pv
        pe_tt = P.at("pe")
        d1 = dve(lambda e, t=tt: e.tensor_copy(rowa[0:1, t * TT:(t + 1) * TT],
                 ps7[0:1, 0:512]), waits=[("pe", pe_tt)])
        d2 = dve(lambda e, t=tt: e.tensor_copy(rowb[0:1, t * TT:(t + 1) * TT],
                 ps7[0:1, 512:1024]), waits=[("pe", pe_tt)])
        st_ev = d2

    def row_norm_stats(dep_dve, n=S, m2_waits=()):
        d_m = dve(lambda e: e.tensor_scalar(rowa[0:1, 0:n], rowa[0:1, 0:n],
                  1.0 / DM, None, OP.mult), waits=[("dve", dep_dve)])
        d_m2 = dve(lambda e: e.tensor_tensor(m2row[0:1, 0:n], rowa[0:1, 0:n],
                   rowa[0:1, 0:n], OP.mult), waits=[("dve", d_m)] + list(m2_waits))
        d_v = dve(lambda e: e.scalar_tensor_tensor(rowb[0:1, 0:n], rowb[0:1, 0:n],
                  1.0 / DM, m2row[0:1, 0:n], OP.mult, OP.subtract),
                  waits=[("dve", d_m2), ("dve", dep_dve)])
        d_ve = dve(lambda e: e.tensor_scalar(rowb[0:1, 0:n], rowb[0:1, 0:n],
                   EPS, None, OP.add), waits=[("dve", d_v)])
        a_l = act(lambda e: e.activation(rowb[0:1, 0:n], rowb[0:1, 0:n], AF.Ln,
                  bias=0.0, scale=1.0), waits=[("dve", d_ve)])
        a_e = act(lambda e: e.activation(rowb[0:1, 0:n], rowb[0:1, 0:n], AF.Exp,
                  bias=0.0, scale=-0.5), waits=[("act", a_l)])
        return a_e

    a_rstd = row_norm_stats(st_ev)

    def bcast_rows(mdst, rdst, a_dep, blocks=NTT, bw=TT, extra=()):
        ut_free = P.at("act")
        last_a = None
        for (row, dst) in ((rowa, mdst), (rowb, rdst)):
            for b in range(blocks):
                pv = mm(UT, negb[0:1, 0:128], row[0:1, b * bw:(b + 1) * bw],
                        start=True, stop=True,
                        waits=[("act", a_dep), ("act", ut_free), ("pool", c_mem)],
                        inc="pe")
                last_a = act(lambda e, b=b, d=dst, w=bw: e.activation(
                    d[:, b * w:(b + 1) * w], UT[:, 0:w], AF.Copy, bias=0.0,
                    scale=-1.0), waits=[("pe", pv)] + list(extra))
                ut_free = last_a
        return last_a

    a_bc1 = bcast_rows(mbt, rbt, a_rstd)

    tscr = [av(o_psb, 128, S), av(o_psb + 4096, 128, S)]
    h_marks = []
    for dc in range(NDC):
        sl = tscr[dc % 2]
        wv_ = [("act", a_bc1)]
        if dc >= 2:
            wv_.append(("dve", h_marks[dc - 2]))
        d1 = dve(lambda e, d=dc, s=sl: e.tensor_tensor(s, xh(d), mbt[:, :],
                 OP.subtract), waits=wv_)
        d2 = dve(lambda e, d=dc, s=sl: e.tensor_tensor(xh(d), s, rbt[:, :], OP.mult),
                 waits=[("dve", d1)])
        h_marks.append(d2)
    h_all = h_marks[-1]

    if phase_limit == "p1":
        for dc in range(NDC):
            dma("qgp", "sdbg", out[dc * 128:(dc + 1) * 128, :], xh(dc),
                waits=[("dve", h_all)])
        return _finish(nc, es, P, sems)

    P.tag = "p23setup"
    wbuf = [av(o_wbuf, 128, 4096), av(o_wbuf + 8192, 128, 4096)]
    kK = av(o_kvq, 128, S)
    kV = av(o_kvq + 4096, 128, S)
    kQ = [av(o_kvq + 8192, 128, S), av(o_kvq + 12288, 128, S)]
    psb = [av(o_psb, 128, S), av(o_psb + 4096, 128, S)]
    ptr4 = [av(o_psb + j * 1024, 128, 512) for j in range(4)]
    ctxh = av(o_ctxh, 128, S)
    ropeA = av(o_ropeA, 128, 1024)
    ropeB = av(o_ropeB, 128, 1024)
    nbc = av(o_nbc, 128, 16, F32)
    rs4 = av(o_rs, 128, 8, F32)
    rsb4 = av(o_rc, 128, 8)
    rs5s = [av(o_rs5, 128, 1, F32), av(o_rs5 + 4, 128, 1, F32)]
    rcb = av(o_rcb, 128, 512, F32)
    rrow = m2row
    bkr = av(o_bkr, 1, 128)
    bvr = av(o_bvr, 1, 128)
    bqr = av(o_bqr, 1, 512)

    wb_free = [[("dve", h_all)], [("dve", h_all)]]
    wb_loaded = [0, 0]
    slot_turn = [0]
    acc_free = [[("dve", 0)], [("dve", 0)]]
    tp_free = [("act", 0)]
    pt_free = [("pe", 0)]
    pbuf_free = [[("dve", h_all)], [("dve", h_all)]]
    ctxh_free = [("qsp", 0)]
    nbc_free = [("act", 0)]
    rowa_free = [("act", a_bc1)]
    rowb_free = [("act", a_bc1)]
    scrA_free = [("dve", 0)]
    scrB_free = [("dve", 0)]
    kmax_mark = [0]
    kv_mark = [0]
    ctx_evac = {0: ("dve", 0), 1: ("dve", 0)}
    q_slot_attn_pe = [0, 0]
    pring = [0]
    stb_turn = [0]
    st_banks = [ps7[:, 0:512], ps7[:, 512:1024], ps7[:, 1024:1536],
                ps7[:, 1536:2048]]
    st_free = {i: [("dve", 0)] for i in range(4)}
    pt_free4 = {i: [("dve", h_all)] for i in range(4)}
    rsum_ps = tpb.ap()[0:1, 0:1024].bitcast(F32)
    rs_bank_free = [[("act", 0)]]
    kq2f = av(o_rs5 + 8, 128, 1, F32)
    kq2b = av(o_rs5 + 12, 128, 2)
    ut_mm = [0]
    ut_free_a = [("act", 0)]
    rrow_free = [("pe", 0)]
    rcb_free = [("dve", 0)]

    def load_slab(src_ap):
        sl = slot_turn[0] % 2
        slot_turn[0] += 1
        sname = "swb%d" % sl
        v = dma("qgp", sname, wbuf[sl][:, :], src_ap, waits=wb_free[sl])
        wb_loaded[sl] = (sname, v)
        return sl

    acc_turn = [0]

    def proj_chunk(sl, bias_row, evac_fn, extra=()):
        last_pv = None
        for tt in range(NTT):
            bi = acc_turn[0] % 2
            acc_turn[0] += 1
            bank = [CXA, CXB][bi]
            for kc in range(NDC):
                wv_ = [wb_loaded[sl], ("dve", h_marks[kc])] + list(extra)
                if kc == 0:
                    wv_ = wv_ + acc_free[bi]
                mm(bank, wbuf[sl][:, kc * 128:(kc + 1) * 128],
                   xh(kc)[:, tt * TT:(tt + 1) * TT], start=(kc == 0), stop=False,
                   waits=wv_)
            pv = mm(bank, bias_row, onesr[0:1, 0:TT], start=False, stop=True,
                    waits=[("pool", c_mem)], inc="pe")
            mk = evac_fn(bank, pv, tt)
            acc_free[bi] = [("dve", mk)]
            last_pv = pv
        return last_pv

    def rope_inplace(dst, pre_waits=(), tt_ready=None):
        marks = []
        for hf in range(2):
            cs = slice(hf * 1024, hf * 1024 + 1024)
            wv0 = scrA_free + scrB_free + list(pre_waits) + [c_const]
            if tt_ready is not None:
                wv0 = wv0 + [tt_ready[2 * hf], tt_ready[2 * hf + 1]]
            dve(lambda e, cs=cs: e.tensor_tensor(ropeA[0:64, :], dst[0:64, cs],
                cosb[0:64, cs], OP.mult), waits=wv0)
            dve(lambda e, cs=cs: e.tensor_tensor(ropeA[64:128, :], dst[0:64, cs],
                sinb[0:64, cs], OP.mult))
            dve(lambda e, cs=cs: e.tensor_tensor(ropeB[0:64, :], dst[64:128, cs],
                sinb[64:128, cs], OP.mult))
            e4 = dve(lambda e, cs=cs: e.tensor_tensor(ropeB[64:128, :],
                     dst[64:128, cs], cosb[64:128, cs], OP.mult))
            dve(lambda e, cs=cs: e.tensor_tensor(dst[0:64, cs], ropeA[0:64, :],
                ropeB[0:64, :], OP.subtract), waits=[("dve", e4)])
            e_last = dve(lambda e, cs=cs: e.tensor_tensor(dst[64:128, cs],
                         ropeA[64:128, :], ropeB[64:128, :], OP.add),
                         waits=[("dve", e4)])
            scrA_free[0] = ("dve", e_last)
            scrB_free[0] = ("dve", e_last)
            marks.append(e_last)
        return marks

    def sq_row_mms(srcbuf, src_marks, extra=()):
        sq_slots = [ropeA[:, 0:TT], ropeB[:, 0:TT]]
        slot_w = [scrA_free, scrB_free]
        out = []
        for tt in range(NTT):
            i = tt % 2
            dd = dve(lambda e, s=sq_slots[i], t=tt: e.tensor_tensor(
                s, srcbuf[:, t * TT:(t + 1) * TT], srcbuf[:, t * TT:(t + 1) * TT],
                OP.mult), waits=[("dve", src_marks[tt])] + slot_w[i])
            pv = mm(ps7[0:1, tt * TT:(tt + 1) * TT], onescolb[:, :], sq_slots[i],
                    start=True, stop=True, waits=[("dve", dd)] + list(extra),
                    inc="pe")
            slot_w[i][0] = ("pe", pv)
            out.append(pv)
        return out

    def attention_head(g, hh, qbuf, q_marks):
        hg = 4 * g + hh
        P.tag = "attn_g%d_h%d" % (g, hh)
        qm = {0: ("dve", q_marks[0]), 1: ("dve", q_marks[0]),
              2: ("dve", q_marks[1]), 3: ("dve", q_marks[1])}
        sq_pe = sq_row_mms(qbuf, {t: qm[t][1] for t in range(4)},
                           extra=st_free[0] + st_free[1] + st_free[2]
                           + st_free[3] + [rowa_free[0]])
        d_ev = dve(lambda e: e.tensor_copy(rowa[0:1, 0:2048], ps7[0:1, 0:2048]),
                   waits=[("pe", sq_pe[3]), rowa_free[0]])
        d_mx = dve(lambda e: e.reduce_max(kq2f[0:1, 0:1], rowa[0:1, 0:2048],
                   axis=AX.X), waits=[("dve", d_ev)])
        rowa_free[0] = ("dve", d_mx)
        d_u1 = dve(lambda e: e.tensor_scalar(kq2f[0:1, 0:1], kq2f[0:1, 0:1],
                   kmaxf[0:1, 0:1], 1e-20, OP.mult, OP.add),
                   waits=[("dve", d_mx), ("dve", kmax_mark[0])])
        a_u2 = act(lambda e: e.activation(kq2f[0:1, 0:1], kq2f[0:1, 0:1], AF.Ln,
                   bias=0.0, scale=1.0), waits=[("dve", d_u1)])
        a_u3 = act(lambda e: e.activation(kq2f[0:1, 0:1], kq2f[0:1, 0:1], AF.Exp,
                   bias=0.0, scale=0.5), waits=[("act", a_u2)])
        d_u4 = dve(lambda e: e.tensor_copy(kq2b[0:1, 0:1], kq2f[0:1, 0:1]),
                   waits=[("act", a_u3)])
        nb_mm = mm(UT[:, 0:1], negb[0:1, 0:128], kq2b[0:1, 0:1], start=True,
                   stop=True, waits=[("dve", d_u4), ("pool", c_mem), ut_free_a[0]],
                   inc="pe")
        a_nb = act(lambda e: e.activation(nbc[:, 0:1], UT[:, 0:1], AF.Copy,
                   bias=EXPC, scale=1.0), waits=[("pe", nb_mm), nbc_free[0]])
        nbc_free[0] = ("act", a_nb)
        ut_free_a[0] = ("act", a_nb)

        units = []
        for qg in range(4):
            for kc in range(4 * qg + 4):
                units.append((qg, kc))
        DELAY = 3
        st = {}

        def stage1(j):
            (qg, kc) = units[j]
            P.tag = "attn_g%d_h%d_u%d_%d" % (g, hh, qg, kc)
            sb = stb_turn[0] % 4
            stb_turn[0] += 1
            reg = st_banks[sb]
            wv_ = (st_free[sb] + [qm[min(3, kc // 4)], ("dve", kv_mark[0]),
                                  ("dve", d_ev), c_const, qm[qg]])
            diag = (kc // 4 == qg) and (kc % 4 <= 3)
            qt_of_kc = kc
            in_group = (4 * qg <= kc <= 4 * qg + 3)
            lastm = mm(reg, kK[:, kc * 128:(kc + 1) * 128],
                       qbuf[:, qg * TT:(qg + 1) * TT],
                       start=True, stop=not in_group, waits=wv_,
                       inc=None if in_group else "pe")
            if in_group:
                w = (kc - 4 * qg) * 128
                lastm = mm(reg[:, w:w + 128], identb[:, :], tribT[:, :],
                           start=False, stop=True, waits=[], inc="pe")
            pj = pring[0] % 4
            pring[0] += 1
            ptile = ptr4[pj]
            av_ = act(lambda e, r=reg, p=ptile, b=nbc[:, 0:1]: e.activation(
                p[:, :], r[:, :], AF.Exp, bias=b, scale=1.0),
                waits=[("pe", lastm), ("act", a_nb)] + pt_free4[pj])
            st_free[sb] = [("act", av_)]
            zm = None
            nz = kc - 4 * qg
            if nz > 0:
                zm = P.op("pool", lambda e, p=ptile, n=nz * 128: e.memset(
                    p[:, 0:n], 0.0), waits=[("act", av_)], inc="pool")
            st[j] = dict(qg=qg, kc=kc, pj=pj, av=av_, zm=zm)

        def stage2(j):
            s = st.pop(j)
            qg, kc, pj = s["qg"], s["kc"], s["pj"]
            P.tag = "attn_g%d_h%d_u%d_%dB" % (g, hh, qg, kc)
            ptile = ptr4[pj]
            first = (kc == 0)
            last = (kc == 4 * qg + 3)
            cb = qg % 2
            bank = [CXA, CXB][cb]
            rw = [("act", s["av"])] + ([("pool", s["zm"])] if s["zm"] else [])
            mm(rsum_ps, onescolb[:, :], ptile[:, :], start=first, stop=last,
               waits=rw + (rs_bank_free[0] if first else []), inc=None)
            pvw = list(rw)
            if first:
                pvw += acc_free[cb] + [ctx_evac[cb]]
            pv = mm(bank, kV[:, kc * 128:(kc + 1) * 128], ptile[:, :],
                    start=first, stop=last, waits=pvw, inc="pe")
            pt_free4[pj] = [("pe", pv)]
            if last:
                d_rr = dve(lambda e: e.reciprocal(rcb[0:1, 0:512],
                           rsum_ps[0:1, 0:512]),
                           waits=[("pe", pv), rcb_free[0]])
                rs_bank_free[0] = [("dve", d_rr)]
                d_rr2 = dve(lambda e: e.tensor_copy(rrow[0:1, 0:512],
                            rcb[0:1, 0:512]), waits=[("dve", d_rr), rrow_free[0]])
                b_mm = mm(UT[:, 0:512], negb[0:1, 0:128], rrow[0:1, 0:512],
                          start=True, stop=True,
                          waits=[("dve", d_rr2), ut_free_a[0]], inc="pe")
                a_rcb = act(lambda e: e.activation(rcb[:, :], UT[:, 0:512],
                            AF.Copy, bias=0.0, scale=-1.0),
                            waits=[("pe", b_mm), rcb_free[0]])
                ut_free_a[0] = ("act", a_rcb)
                rrow_free[0] = ("pe", b_mm)
                d_cx = dve(lambda e, c=bank, q0=qg: e.tensor_tensor(
                    ctxh[:, q0 * TT:(q0 + 1) * TT], c, rcb[:, :], OP.mult),
                    waits=[("pe", pv), ("act", a_rcb), ctxh_free[0]])
                rcb_free[0] = ("dve", d_cx)
                ctx_evac[cb] = ("dve", d_cx)
                acc_free[cb] = [("dve", d_cx)]

        for j in range(len(units)):
            stage1(j)
            if j >= DELAY:
                stage2(j - DELAY)
        for j in range(len(units) - DELAY, len(units)):
            stage2(j)
        v = dma("qsp", "sctx", ctx_scr[hg * 128:(hg + 1) * 128, :], ctxh[:, :],
                waits=[ctx_evac[0], ctx_evac[1]])
        ctxh_free[0] = ("sctx", v)
        return v

    ctx_spills = []
    for g in range(NKV):
        P.tag = "grp%d" % g
        dma("qgp", "sbias", bkr, bk_in[0:1, g * 128:(g + 1) * 128],
            waits=[("pe", q_slot_attn_pe[0]), ("pe", q_slot_attn_pe[1])])
        dma("qgp", "sbias", bvr, bv_in[0:1, g * 128:(g + 1) * 128])
        dma("qgp", "sbias", bqr, bq_in[0:1, g * 512:(g + 1) * 512])
        bias_q = P.at("sbias")

        sl = load_slab(wk_in[g * 128:(g + 1) * 128, :])

        def evac_k(bank, pv, tt):
            return dve(lambda e, b=bank, t=tt: e.tensor_copy(
                kK[:, t * TT:(t + 1) * TT], b),
                waits=[("pe", pv), ("pe", q_slot_attn_pe[0]),
                       ("pe", q_slot_attn_pe[1])])

        lp = proj_chunk(sl, bkr[0:1, :], evac_k, extra=[("sbias", bias_q)])
        wb_free[sl] = [("pe", lp)]
        d_kropes = rope_inplace(kK, pre_waits=[("pe", q_slot_attn_pe[0]),
                                               ("pe", q_slot_attn_pe[1])])
        d_krope = d_kropes[1]

        sl = load_slab(wv_in[g * 128:(g + 1) * 128, :])
        vf = psb[0]

        def evac_v(bank, pv, tt):
            return dve(lambda e, b=bank, t=tt: e.tensor_copy(
                vf[:, t * TT:(t + 1) * TT], b),
                waits=[("pe", pv)] + pt_free4[0] + pt_free4[1]
                + pt_free4[2] + pt_free4[3])

        lp = proj_chunk(sl, bvr[0:1, :], evac_v, extra=[("sbias", bias_q)])
        wb_free[sl] = [("pe", lp)]
        v_fm = P.at("dve")
        v_tr = []
        t_last = None
        for b0 in range(0, 16, 8):
            for kc in range(b0, b0 + 8):
                t_last = P.op("pe", lambda e, k=kc, j=(kc - b0): e.transpose(
                    tpb[:, j * 128:(j + 1) * 128], vf[:, k * 128:(k + 1) * 128],
                    identb[:, :]),
                    waits=[("dve", v_fm), c_const, tp_free[0]]
                    + rs_bank_free[0],
                    inc="pe" if kc == b0 + 7 else None)
            a_v = act(lambda e, b0=b0: e.copy(kV[:, b0 * 128:(b0 + 8) * 128],
                      tpb[:, 0:1024]), waits=[("pe", t_last)])
            tp_free[0] = ("act", a_v)
            rs_bank_free[0] = [("act", a_v)]
            v_tr.append(a_v)
        for _j in range(4):
            pt_free4[_j] = [("pe", t_last)]

        km = {0: d_kropes[0], 1: d_kropes[0], 2: d_kropes[1], 3: d_kropes[1]}
        pe_k2 = sq_row_mms(kK, km,
                           extra=st_free[0] + st_free[1] + st_free[2]
                           + st_free[3] + [rowb_free[0]])
        d_ke = dve(lambda e: e.tensor_copy(rowb[:, :], ps7[0:1, 0:2048]),
                   waits=[("pe", pe_k2[-1]), rowb_free[0]])
        d_km = dve(lambda e: e.reduce_max(kmaxf[0:1, 0:1], rowb[:, :], axis=AX.X),
                   waits=[("dve", d_ke)])
        kmax_mark[0] = d_km
        rowb_free[0] = ("dve", d_km)
        d_kv = dve(lambda e: e.engine_nop(),
                   waits=[("dve", d_krope), ("act", v_tr[-1])])
        kv_mark[0] = d_kv

        for hh in range(RQ):
            qs = hh % 2
            qb = kQ[qs]
            sl = load_slab(wq_in[(4 * g + hh) * 128:(4 * g + hh + 1) * 128, :])

            qev = {}

            def evac_q2(bank, pv, tt, q=qb, qs=qs):
                d = dve(lambda e, b=bank, t=tt, q=q: e.tensor_scalar(
                    q[:, t * TT:(t + 1) * TT], b, QSCALE, None, OP.mult),
                    waits=[("pe", pv), ("pe", q_slot_attn_pe[qs])])
                qev[tt] = ("dve", d)
                return d

            lp = proj_chunk(sl, bqr[0:1, hh * 128:(hh + 1) * 128], evac_q2,
                            extra=[("sbias", bias_q)])
            wb_free[sl] = [("pe", lp)]
            d_qrope = rope_inplace(qb, pre_waits=[("pe", q_slot_attn_pe[qs])],
                                   tt_ready=qev)
            v = attention_head(g, hh, qb, d_qrope)
            ctx_spills.append(v)
            q_slot_attn_pe[qs] = P.at("pe")

        rowb_free[0] = ("dve", P.at("dve"))

    pe_attn_end = P.at("pe")
    dve_attn_end = P.at("dve")
    act_attn_end = P.at("act")

    P.tag = "p4"
    for dc in range(NDC):
        dma("qsp", "scxr", xh(dc), ctx_scr[dc * 128:(dc + 1) * 128, :],
            waits=[("pe", pe_attn_end), ("dve", dve_attn_end),
                   ("sctx", P.at("sctx"))])
    ctx_rd_all = P.at("scxr")
    bor = av(o_kvq, 1, DM)
    v_bo = dma("qgp", "sbias", bor, bo_in[0:1, :], waits=[("pe", pe_attn_end)])
    apst = [psb[0], psb[1]]
    ap_spill = []
    apst_free = [[("act", act_attn_end), ("pe", pe_attn_end)],
                 [("act", act_attn_end), ("pe", pe_attn_end)]]
    for ob in range(NDC):
        sl = load_slab(wo_in[ob * 128:(ob + 1) * 128, :])
        st = ob % 2
        dvl = None
        for tt in range(NTT):
            bi = acc_turn[0] % 2
            acc_turn[0] += 1
            bank = [CXA, CXB][bi]
            for kc in range(NDC):
                wv_ = [wb_loaded[sl], ("scxr", ctx_rd_all), ("sbias", v_bo)]
                if kc == 0:
                    wv_ = wv_ + acc_free[bi]
                mm(bank, wbuf[sl][:, kc * 128:(kc + 1) * 128],
                   xh(kc)[:, tt * TT:(tt + 1) * TT], start=(kc == 0), stop=False,
                   waits=wv_)
            pv = mm(bank, bor[0:1, ob * 128:(ob + 1) * 128], onesr[0:1, 0:TT],
                    start=False, stop=True, inc="pe")
            dvl = dve(lambda e, b=bank, t=tt, s=st: e.tensor_copy(
                apst[s][:, t * TT:(t + 1) * TT], b),
                waits=[("pe", pv)] + apst_free[st])
            acc_free[bi] = [("dve", dvl)]
            wb_free[sl] = [("pe", pv)]
        v = dma("qsp", "sap%d" % st, ap_scr[ob * 128:(ob + 1) * 128, :],
                apst[st][:, :], waits=[("dve", dvl)])
        ap_spill.append(v)
        apst_free[st] = [("sap%d" % st, v)]

    pe_p4_end = P.at("pe")
    ap_all = [("sap0", P.at("sap0")), ("sap1", P.at("sap1"))]

    P.tag = "p5"
    o5_apb = o_wbuf
    o5_sqs = o_wbuf + 8192
    o5_xsb = o_wbuf + 10240
    o5_tsc = o_wbuf + 26624
    o5_xps = o_wbuf + 34816
    o5_mb = o_wbuf + 43008
    o5_rb = o_wbuf + 47104
    assert o_wbuf + 51200 <= o_rowa
    apb = [av(o5_apb, 128, S), av(o5_apb + 4096, 128, S)]
    sqs5 = [av(o5_sqs, 128, TT), av(o5_sqs + 1024, 128, TT)]
    xsb = [av(o5_xsb, 128, S, F32), av(o5_xsb + 8192, 128, S, F32)]
    tsc5 = [av(o5_tsc, 128, S), av(o5_tsc + 4096, 128, S)]
    xps5 = [av(o5_xps, 128, S), av(o5_xps + 4096, 128, S)]
    mbt2 = av(o5_mb, 128, S)
    rbt2 = av(o5_rb, 128, S)

    def stats_ln2(pre_waits):
        apb2 = [av(o_kvq + 8192, 128, S), av(o_kvq + 12288, 128, S)]
        sqs2 = [av(o_ropeA, 128, TT), av(o_ropeB, 128, TT)]
        ev_last = 0
        buf_free = [list(pre_waits), list(pre_waits)]
        sq_free = [list(pre_waits), list(pre_waits)]
        for half in range(2):
            for dc in range(NDC):
                i = dc % 2
                v = dma("qsp", "sab%d" % i, apb2[i],
                        ap_scr[dc * 128:(dc + 1) * 128, :],
                        waits=buf_free[i] + [("sap%d" % (dc % 2), ap_spill[dc])])
                for tt in (half * 2, half * 2 + 1):
                    sum_ps = ps7[0:1, (tt % 2) * TT:(tt % 2 + 1) * TT]
                    sq_ps = ps7[0:1, 1024 + (tt % 2) * TT:1024 + (tt % 2 + 1) * TT]
                    wv_ = [("sab%d" % i, v)]
                    if dc == 0:
                        wv_ += list(pre_waits) + [("dve", ev_last)]
                    mm(sum_ps, onescolb[:, :], apb2[i][:, tt * TT:(tt + 1) * TT],
                       start=(dc == 0), stop=(dc == NDC - 1), waits=wv_,
                       inc="pe" if dc == NDC - 1 else None)
                    si = dc % 2
                    dd = P.op("pool", lambda e, s=sqs2[si], b=apb2[i], t=tt:
                              e.tensor_tensor(s, b[:, t * TT:(t + 1) * TT],
                              b[:, t * TT:(t + 1) * TT], OP.mult),
                              waits=[("sab%d" % i, v)] + sq_free[si], inc="pool")
                    pv2 = mm(sq_ps, onescolb[:, :], sqs2[si], start=(dc == 0),
                             stop=(dc == NDC - 1),
                             waits=[("pool", dd)] + (wv_ if dc == 0 else []),
                             inc="pe")
                    sq_free[si] = [("pe", pv2)]
                buf_free[i] = [("pe", P.at("pe"))]
            pe_h = P.at("pe")
            for tt in (half * 2, half * 2 + 1):
                dve(lambda e, t=tt: e.tensor_copy(rowa[0:1, t * TT:(t + 1) * TT],
                    ps7[0:1, (t % 2) * TT:(t % 2 + 1) * TT]), waits=[("pe", pe_h)])
                ev_last = dve(lambda e, t=tt: e.tensor_copy(
                    rowb[0:1, t * TT:(t + 1) * TT],
                    ps7[0:1, 1024 + (t % 2) * TT:1024 + (t % 2 + 1) * TT]),
                    waits=[("pe", pe_h)])
        return ev_last

    pre_w = [("pe", pe_attn_end), ("act", act_attn_end), ("dve", dve_attn_end),
             rowa_free[0], rowb_free[0]]
    s_ev = stats_ln2(pre_w)
    if phase_limit == "rawstats":
        dma("qgp", "sdbg", out[256:257, :], rowb[0:1, :], waits=[("dve", s_ev)])
        dma("qgp", "sdbg", out[257:258, :], rowa[0:1, :], waits=[("dve", s_ev)])
        return _finish(nc, es, P, sems)
    a_r2 = row_norm_stats(s_ev)
    a_bc2 = bcast_rows(mbt2, rbt2, a_r2,
                       extra=[("pe", pe_p4_end), ("sctx", P.at("sctx"))] + ap_all)
    rowa_free[0] = ("act", a_bc2)
    rowb_free[0] = ("act", a_bc2)

    if phase_limit == "ln2":
        dma("qgp", "sdbg", out[0:128, :], rbt2, waits=[("act", a_bc2)])
        dma("qgp", "sdbg", out[128:256, :], mbt2, waits=[("act", a_bc2)])
        dma("qgp", "sdbg", out[256:257, :], rowb[0:1, :], waits=[("act", a_bc2)])
        dma("qgp", "sdbg", out[257:258, :], rowa[0:1, :], waits=[("act", a_bc2)])
        return _finish(nc, es, P, sems)

    xp_spill = []
    b_free = [[("act", a_bc2), ("pe", P.at("pe"))],
              [("act", a_bc2), ("pe", P.at("pe"))]]
    x_free = [[("pe", pe_p4_end)], [("pe", pe_p4_end)]]
    t_free = [[("pe", pe_p4_end), ("dve", P.at("dve"))],
              [("pe", pe_p4_end), ("dve", P.at("dve"))]]
    xp_free = [ap_all + [("pe", pe_p4_end)],
               ap_all + [("pe", pe_p4_end)]]
    sq3_banks = [UT[0:1, 0:512], CXA[0:1, 0:512], CXB[0:1, 0:512],
                 tpb.ap()[0:1, 0:1024].bitcast(F32)[0:1, 0:512]]
    sqp_free = [[("pe", P.at("pe")), ("act", act_attn_end)],
                [("pe", P.at("pe")), ("act", act_attn_end)]]
    st3_pre = [("act", a_bc2), ("dve", P.at("dve")), ("pe", pe_p4_end)]
    last_sum3 = {}
    for dc in range(NDC):
        i = dc % 2
        v1 = dma("qsp", "sab%d" % i, apb[i], ap_scr[dc * 128:(dc + 1) * 128, :],
                 waits=b_free[i] + ap_all)
        v2 = dma("qsp", "sxf%d" % i, xsb[i], x_in[dc * 128:(dc + 1) * 128, :],
                 waits=x_free[i])
        d1 = dve(lambda e, i=i: e.tensor_tensor(tsc5[i], apb[i], mbt2[:, :],
                 OP.subtract), waits=[("sab%d" % i, v1), ("act", a_bc2)] + t_free[i])
        d2 = dve(lambda e, i=i: e.tensor_tensor(xps5[i], tsc5[i], rbt2[:, :],
                 OP.mult), waits=[("dve", d1)] + xp_free[i])
        d3 = dve(lambda e, i=i: e.tensor_tensor(xps5[i], xps5[i], xsb[i], OP.add),
                 waits=[("dve", d2), ("sxf%d" % i, v2)])
        v3 = dma("qsp", "sxp%d" % i, xp_scr[dc * 128:(dc + 1) * 128, :], xps5[i],
                 waits=[("dve", d3)])
        xp_spill.append(v3)
        for tt in range(NTT):
            wv_ = [("dve", d3)] + (st3_pre if dc == 0 else [])
            last_sum3[tt] = mm(ps7[0:1, tt * TT:(tt + 1) * TT], onescolb[:, :],
                               xps5[i][:, tt * TT:(tt + 1) * TT],
                               start=(dc == 0), stop=(dc == NDC - 1), waits=wv_,
                               inc="pe" if dc == NDC - 1 else None)
            sp = tt % 2
            pp = P.op("pool", lambda e, s=sqs5[sp], i=i, t=tt: e.tensor_tensor(
                s, xps5[i][:, t * TT:(t + 1) * TT],
                xps5[i][:, t * TT:(t + 1) * TT], OP.mult),
                waits=[("dve", d3)] + sqp_free[sp], inc="pool")
            pq = mm(sq3_banks[tt], onescolb[:, :], sqs5[sp],
                    start=(dc == 0), stop=(dc == NDC - 1),
                    waits=[("pool", pp)] + (st3_pre if dc == 0 else []),
                    inc="pe")
            sqp_free[sp] = [("pe", pq)]
        b_free[i] = [("dve", d1)]
        x_free[i] = [("dve", d3)]
        t_free[i] = [("dve", d2)]
        xp_free[i] = [("sxp%d" % i, v3), ("pe", P.at("pe"))]

    xp_all = [("sxp0", P.at("sxp0")), ("sxp1", P.at("sxp1"))]
    pe_st3 = P.at("pe")
    s_ev3 = 0
    for tt in range(NTT):
        dve(lambda e, t=tt: e.tensor_copy(rowa[0:1, t * TT:(t + 1) * TT],
            ps7[0:1, t * TT:(t + 1) * TT]), waits=[("pe", pe_st3), rowa_free[0]])
        s_ev3 = dve(lambda e, t=tt: e.tensor_copy(rowb[0:1, t * TT:(t + 1) * TT],
                    sq3_banks[t]), waits=[("pe", pe_st3), rowb_free[0]])
    a_r3 = row_norm_stats(s_ev3)
    a_bc3 = bcast_rows(mbt2, rbt2, a_r3)
    rowa_free[0] = ("act", a_bc3)
    rowb_free[0] = ("act", a_bc3)

    h2_marks = []
    hb_free = [[("act", a_bc3), ("pe", P.at("pe"))],
               [("act", a_bc3), ("pe", P.at("pe"))]]
    t2_free = [[("dve", P.at("dve"))], [("dve", P.at("dve"))]]
    for dc in range(NDC):
        i = dc % 2
        v1 = dma("qsp", "sab%d" % i, apb[i], xp_scr[dc * 128:(dc + 1) * 128, :],
                 waits=hb_free[i] + xp_all)
        wv_ = [("sab%d" % i, v1), ("act", a_bc3), ("pe", pe_p4_end)] + t2_free[i]
        if dc >= 2:
            wv_.append(("dve", h2_marks[dc - 2]))
        d1 = dve(lambda e, i=i: e.tensor_tensor(tsc5[i], apb[i], mbt2[:, :],
                 OP.subtract), waits=wv_)
        d2 = dve(lambda e, i=i, d=dc: e.tensor_tensor(xh(d), tsc5[i], rbt2[:, :],
                 OP.mult), waits=[("dve", d1)])
        h2_marks.append(d2)
        hb_free[i] = [("dve", d1)]
        t2_free[i] = [("dve", d2)]
    h2_all = h2_marks[-1]

    if phase_limit == "h2":
        for dc in range(NDC):
            dma("qgp", "sdbg", out[dc * 128:(dc + 1) * 128, :], xh(dc),
                waits=[("dve", h2_all)])
        return _finish(nc, es, P, sems)

    P.tag = "p6"
    o6_w = o_wbuf
    o6_wdc = o_wbuf + 32768
    o6_g = o6_wdc + 22016
    o6_u = o6_g + 2048
    o6_m = o6_u + 2048
    assert o6_m + 8192 <= AB
    wb6 = [av(o6_w + i * 8192, 128, 4096) for i in range(4)]
    wdc = av(o6_wdc, 128, FF)
    gst = [av(o6_g, 128, TT), av(o6_g + 1024, 128, TT)]
    ust = [av(o6_u, 128, TT), av(o6_u + 1024, 128, TT)]
    mst = [av(o6_m, 128, S), av(o6_m + 4096, 128, S)]

    wb6_free = [[("pe", P.at("pe")), ("dve", P.at("dve"))] for _ in range(4)]
    m_spills = []
    mst_free = [[("dve", P.at("dve")), ("pe", P.at("pe"))],
                [("dve", P.at("dve")), ("pe", P.at("pe"))]]
    g_free = [[("dve", P.at("dve")), ("pe", P.at("pe"))],
              [("dve", P.at("dve")), ("pe", P.at("pe"))]]
    u_free = [[("dve", P.at("dve")), ("pe", P.at("pe"))],
              [("dve", P.at("dve")), ("pe", P.at("pe"))]]
    acc_free6 = [[("dve", s_ev3)], [("dve", s_ev3)]]
    wdc_q = [0]
    wd_conv = []
    m_dst = m_scr.rearrange("t p j -> p t j")

    for fb in range(NFC):
        P.tag = "p6_f%d" % fb
        s0 = (fb % 2) * 2
        sg = ("sg0", "sg1")[fb % 2]
        su = ("su0", "su1")[fb % 2]
        v_g = dma("qgp", sg, wb6[s0], wg_in[fb * 128:(fb + 1) * 128, :],
                  waits=wb6_free[s0] + ([("dve", h2_all)] if fb == 0 else []))
        v_u = dma("qgp", su, wb6[s0 + 1], wu_in[fb * 128:(fb + 1) * 128, :],
                  waits=wb6_free[s0 + 1])
        while len(wd_conv) * NFC <= fb * NDC and len(wd_conv) < NDC:
            db = len(wd_conv)
            vv = dma("qgp", "scv", wdc, wd_in[db * 128:(db + 1) * 128, :],
                     waits=[("scw", wdc_q[0])] + ([("pe", pe_p4_end),
                     ("dve", P.at("dve"))] if db == 0 else []))
            vw = dma("qsp", "scw", wd_scr[db * 128:(db + 1) * 128, :], wdc,
                     waits=[("scv", vv)])
            wdc_q[0] = vw
            wd_conv.append(vw)
        gp = None
        up = None
        for tt in range(NTT):
            i = tt % 2
            for kc in range(NDC):
                wv_ = [(sg, v_g), ("dve", h2_marks[kc])]
                if kc == 0:
                    wv_ = wv_ + acc_free6[0]
                gp = mm(CXA, wb6[s0][:, kc * 128:(kc + 1) * 128],
                        xh(kc)[:, tt * TT:(tt + 1) * TT], start=(kc == 0),
                        stop=(kc == NDC - 1), waits=wv_,
                        inc="pe" if kc == NDC - 1 else None)
            a_g = act(lambda e, i=i: e.activation(gst[i], CXA, AF.Silu, bias=0.0,
                      scale=1.0), waits=[("pe", gp)] + g_free[i])
            acc_free6[0] = [("act", a_g)]
            for kc in range(NDC):
                wv_ = [(su, v_u), ("dve", h2_marks[kc])]
                if kc == 0:
                    wv_ = wv_ + acc_free6[1]
                up = mm(CXB, wb6[s0 + 1][:, kc * 128:(kc + 1) * 128],
                        xh(kc)[:, tt * TT:(tt + 1) * TT], start=(kc == 0),
                        stop=(kc == NDC - 1), waits=wv_,
                        inc="pe" if kc == NDC - 1 else None)
            d_u = dve(lambda e, i=i: e.tensor_copy(ust[i], CXB),
                      waits=[("pe", up)] + u_free[i])
            acc_free6[1] = [("dve", d_u)]
            d_m = dve(lambda e, i=i, t=tt, f=(fb % 2): e.tensor_tensor(
                mst[f][:, t * TT:(t + 1) * TT], gst[i], ust[i], OP.mult),
                waits=[("act", a_g), ("dve", d_u)] + mst_free[fb % 2])
            g_free[i] = [("dve", d_m)]
            u_free[i] = [("dve", d_m)]
        wb6_free[s0] = [("pe", gp)]
        wb6_free[s0 + 1] = [("pe", up)]
        v_m = dma("qsp", ("sm0", "sm1")[fb % 2],
                  m_dst[:, :, fb * TT:(fb + 1) * TT], mst[fb % 2],
                  waits=[("dve", P.at("dve"))])
        m_spills.append(v_m)
        mst_free[fb % 2] = [(("sm0", "sm1")[fb % 2], v_m), ("dve", P.at("dve"))]
    pe_p6_end = P.at("pe")
    m_all = [("sm0", P.at("sm0")), ("sm1", P.at("sm1"))]
    wd_all = [("scw", P.at("scw"))]

    P.tag = "p7"
    o7_mtt = 0
    o7_wd = 90112
    o7_ytt = 134144
    o7_xps = 166912
    o7_outs = 169984
    o7_sq = 174080
    o7_mb = 177152
    assert o7_mb + 6144 <= o_rowa
    mtt = av(o7_mtt, 128, NFC * TT)
    wd7 = [av(o7_wd, 128, FF), av(o7_wd + 22016, 128, FF)]
    ytt = av(o7_ytt, 128, NDC * TT)
    xps7 = [av(o7_xps, 128, TT), av(o7_xps + 1024, 128, TT)]
    outs7 = [av(o7_outs, 128, TT, F32), av(o7_outs + 2048, 128, TT, F32)]
    sq7 = [av(o7_sq, 128, TT), av(o7_sq + 1024, 128, TT)]
    esq7 = [av(o7_mb + 4096, 128, TT), av(o7_mb + 5120, 128, TT)]
    m4bs = [av(o7_mb, 128, TT, F32), av(o_rowa + 1024, 128, TT, F32)]
    r4bs = [av(o7_mb + 2048, 128, TT, F32), av(o_rowb + 1024, 128, TT, F32)]

    dve_p7s = P.at("dve")
    wd7_free = [[("pe", pe_p6_end)], [("pe", pe_p6_end)]]
    mtt_q = [0]
    ytt_free = [[("pe", pe_p6_end)] for _ in range(NDC)]
    xps7_free = [[("scw", P.at("scw")), ("dve", dve_p7s)],
                 [("scw", P.at("scw")), ("dve", dve_p7s)]]
    outs7_free = [[("scw", P.at("scw")), ("dve", dve_p7s)],
                  [("scw", P.at("scw")), ("dve", dve_p7s)]]
    ut_free7 = [P.at("act")]
    rows_free7 = [P.at("dve")]
    sq7_init = [("scw", P.at("scw")), ("dve", dve_p7s)]

    mb_free = {0: [], 1: []}
    epi_state = [None]
    epi_marks = {}
    epi_ix = [0]

    def plan_epilogue_chunk():
        if epi_state[0] is None:
            return
        (ett, y_marks_e, a_m4e, a_r4e, m4be, r4be) = epi_state[0]
        dc = epi_ix[0]
        if dc >= NDC:
            return
        epi_ix[0] += 1
        i = dc % 2
        v_x = dma("qsp", ("sx70", "sx71")[i], xps7[i],
                  xp_scr[dc * 128:(dc + 1) * 128, ett * TT:(ett + 1) * TT],
                  waits=xps7_free[i] + xp_all)
        dd1 = P.op("pool", lambda e, i=i, d=dc, m=m4be: e.tensor_tensor(
            esq7[i], ytt[:, d * TT:(d + 1) * TT], m, OP.subtract),
            waits=[("act", a_m4e), ("dve", y_marks_e[dc])] + esq_free[i],
            inc="pool")
        dd2 = P.op("pool", lambda e, i=i, r=r4be: e.tensor_tensor(
            esq7[i], esq7[i], r, OP.mult),
            waits=[("pool", dd1), ("act", a_r4e)], inc="pool")
        dd3 = P.op("pool", lambda e, i=i: e.tensor_tensor(
            outs7[i], esq7[i], xps7[i], OP.add),
            waits=[("pool", dd2), (("sx70", "sx71")[i], v_x)] + outs7_free[i],
            inc="pool")
        v_o = dma("qsp", ("so0", "so1")[i],
                  out[dc * 128:(dc + 1) * 128, ett * TT:(ett + 1) * TT],
                  outs7[i], waits=[("pool", dd3)])
        xps7_free[i] = [("pool", dd3)]
        esq_free[i] = [("pool", dd3)]
        outs7_free[i] = [(("so0", "so1")[i], v_o)]
        epi_marks[(ett, dc)] = dd1

    esq_free = [[("scw", P.at("scw"))], [("scw", P.at("scw"))]]

    for tt in range(NTT):
        P.tag = "p7_t%d" % tt
        v_m = dma("qgp", "smt", mtt[:, :], m_scr[tt, :, :],
                  waits=m_all + [("pe", pe_p6_end if tt == 0 else mtt_q[0])])
        sum_ps = ps7[0:1, 2048:2560]
        sq_ps = ps7[0:1, 0:512]
        y_marks = {}
        sq_free7 = [list(sq7_init), list(sq7_init)]
        stats_last = None
        pv = None
        m4b = m4bs[tt % 2]
        r4b = r4bs[tt % 2]
        for db in range(NDC):
            plan_epilogue_chunk()
            sl = db % 2
            v_w = dma("qgp", ("swd0", "swd1")[sl],
                      wd7[sl], wd_scr[db * 128:(db + 1) * 128, :],
                      waits=wd7_free[sl] + wd_all)
            bi = db % 2
            bank = [CXA, CXB][bi]
            for fc in range(NFC):
                wv_ = [(("swd0", "swd1")[sl], v_w), ("smt", v_m)]
                if fc == 0:
                    wv_ = wv_ + acc_free6[bi]
                pv = mm(bank, wd7[sl][:, fc * 128:(fc + 1) * 128],
                        mtt[:, fc * TT:(fc + 1) * TT], start=(fc == 0),
                        stop=(fc == NFC - 1), waits=wv_,
                        inc="pe" if fc == NFC - 1 else None)
            wd7_free[sl] = [("pe", pv)]
            yw = [("pe", pv)] + ytt_free[db]
            if tt > 0:
                yw.append(("pool", epi_marks[(tt - 1, db)]))
            d_y = dve(lambda e, b=bank, d=db: e.tensor_copy(
                ytt[:, d * TT:(d + 1) * TT], b), waits=yw)
            acc_free6[bi] = [("dve", d_y)]
            y_marks[db] = d_y
            mm(sum_ps, onescolb[:, :], ytt[:, db * TT:(db + 1) * TT],
               start=(db == 0), stop=(db == NDC - 1),
               waits=[("dve", d_y)] + ([("act", ut_free7[0]),
               ("dve", rows_free7[0])] if db == 0 else []),
               inc="pe" if db == NDC - 1 else None)
            dd = dve(lambda e, s=sq7[db % 2], d=db: e.tensor_tensor(
                s, ytt[:, d * TT:(d + 1) * TT], ytt[:, d * TT:(d + 1) * TT],
                OP.mult), waits=[("dve", d_y)] + sq_free7[db % 2])
            pv2 = mm(sq_ps, onescolb[:, :], sq7[db % 2], start=(db == 0),
                     stop=(db == NDC - 1),
                     waits=[("dve", dd)] + (st_free[0] if db == 0 and tt == 0
                                            else []),
                     inc="pe")
            sq_free7[db % 2] = [("pe", pv2)]
            stats_last = P.at("pe")
        mtt_q[0] = pv
        d1 = dve(lambda e: e.tensor_copy(rowa[0:1, 0:TT], ps7[0:1, 2048:2560]),
                 waits=[("pe", stats_last), rowa_free[0]])
        d2 = dve(lambda e: e.tensor_copy(rowb[0:1, 0:TT], ps7[0:1, 0:512]),
                 waits=[("pe", stats_last), rowb_free[0]])
        rowa_free[0] = ("dve", d1)
        a_r4r = row_norm_stats(max(d1, d2), n=TT,
                               m2_waits=[("dve", rows_free7[0])])
        pvb = mm(UT, negb[0:1, 0:128], rowa[0:1, 0:TT], start=True, stop=True,
                 waits=[("act", a_r4r), ("act", ut_free7[0]), ("pool", c_mem)],
                 inc="pe")
        a_m4 = act(lambda e, m=m4b: e.activation(m[:, :], UT[:, 0:TT], AF.Copy,
                   bias=0.0, scale=-1.0), waits=[("pe", pvb)] + mb_free[tt % 2])
        pvb2 = mm(UT, negb[0:1, 0:128], rowb[0:1, 0:TT], start=True, stop=True,
                  waits=[("act", a_m4)], inc="pe")
        a_r4 = act(lambda e, r=r4b: e.activation(r[:, :], UT[:, 0:TT], AF.Copy,
                   bias=0.0, scale=-1.0), waits=[("pe", pvb2)] + mb_free[tt % 2])
        ut_free7[0] = a_r4
        rows_free7[0] = P.at("dve")
        rowa_free[0] = ("pe", pvb)
        rowb_free[0] = ("pe", pvb2)
        while epi_state[0] is not None and epi_ix[0] < NDC:
            plan_epilogue_chunk()
        epi_state[0] = (tt, dict(y_marks), a_m4, a_r4, m4b, r4b)
        epi_ix[0] = 0
        mb_free[tt % 2] = []
    while epi_ix[0] < NDC:
        plan_epilogue_chunk()

    return _finish(nc, es, P, sems)


def _finish(nc, es, P, sems):
    with nc.Block() as block:
        @block.tensor
        def _(te):
            P.emit("pe", te, sems)

        @block.scalar
        def _(sc):
            P.emit("act", sc, sems)

        @block.vector
        def _(v):
            P.emit("dve", v, sems)

        @block.gpsimd
        def _(g):
            P.emit("pool", g, sems)

        @block.sync
        def _(sy):
            P.emit("sp", sy, sems)
            for s in P.cnt:
                if s not in ("pe", "act", "dve", "pool") and P.at(s) > 0:
                    sy.wait_ge(sems[s], P.at(s))
    es.close()
    return nc


def kernel(**inputs):
    in_maps = _host_layout(inputs)
    nc = build_nc()
    res = run_bass_kernel_spmd(nc, in_maps, core_ids=list(range(8)))
    outs = []
    for c in range(8):
        outs.append(np.ascontiguousarray(res.results[c]["out"].T))
    return np.stack(outs, 0)
